# version 21 (final)
# baseline (speedup 1.0000x reference)
import math
from contextlib import ExitStack

import numpy as np
import ml_dtypes

import concourse.bass as bass
import concourse.mybir as mybir
from concourse.bass_utils import run_bass_kernel_spmd


F32 = mybir.dt.float32
BF16 = mybir.dt.bfloat16
U8 = mybir.dt.uint8
AF = mybir.ActivationFunctionType
bfdt = ml_dtypes.bfloat16

B, S, D = 2, 8192, 4096
H, DH, DR, DC, DC1 = 32, 128, 64, 512, 1536
NCORES = 8
QOWN = 2048
SCALER = 1.0 / math.sqrt(DH + DR)
EPS = 1e-5
NDT = D // 128
NKT = S // 128
SAME_ENG_SYNC = True


_ENGS = ("pe", "act", "dve", "sp", "gp")
_BLOCKNAME = {"pe": "tensor", "act": "scalar", "dve": "vector", "sp": "sync", "gp": "gpsimd"}


class Buf:

    __slots__ = ("name", "writers_c", "writers_s", "readers_c", "readers_s",
                 "sem", "dma_cum", "sink")

    def __init__(self, name, sink=False):
        self.name = name
        self.writers_c = {}
        self.writers_s = {}
        self.readers_c = {}
        self.readers_s = {}
        self.sem = None
        self.dma_cum = 0
        self.sink = sink


class Sched:
    def __init__(self, nc, ctx):
        self.nc = nc
        self.ctx = ctx
        self.streams = {e: [] for e in _ENGS}
        self.inc_idx = {e: [] for e in _ENGS}
        self.inc_val = {e: [] for e in _ENGS}
        self.tl_sem = {e: None for e in _ENGS}
        self.waited = {e: {} for e in _ENGS}
        self.n_sems = 0

    def _sem(self, name):
        h = self.ctx.enter_context(self.nc.semaphore(name))
        self.n_sems += 1
        return h

    def _tl(self, e):
        if self.tl_sem[e] is None:
            self.tl_sem[e] = self._sem(f"tl_{e}")
        return self.tl_sem[e]

    def _comp_token(self, e, idx):
        import bisect
        ii = self.inc_idx[e]
        j = bisect.bisect_left(ii, idx)
        if j < len(ii):
            return (self._tl(e), self.inc_val[e][j])
        while self.streams[e][idx]["fn"] is None:
            idx += 1
            assert idx < len(self.streams[e]), "token needed past end of stream"
        v = (self.inc_val[e][-1] if self.inc_val[e] else 0) + 1
        self.streams[e][idx]["inc"] = v
        ii.append(idx)
        self.inc_val[e].append(v)
        return (self._tl(e), v)

    def _gather(self, e, reads, writes):
        need_c = {}
        need_s = {}

        def addc(d):
            for ee, ii in d.items():
                if ii > need_c.get(ee, -1):
                    need_c[ee] = ii

        def adds(d):
            for sn, (sem, v) in d.items():
                if v > need_s.get(sn, (None, -1))[1]:
                    need_s[sn] = (sem, v)

        for b in reads:
            addc(b.writers_c)
            adds(b.writers_s)
        for b in writes:
            addc(b.writers_c)
            adds(b.writers_s)
            if not b.sink:
                addc(b.readers_c)
                adds(b.readers_s)
        waits = []
        for ee in sorted(need_c):
            if ee == e and (ee == "pe" or not SAME_ENG_SYNC):
                continue
            sem, v = self._comp_token(ee, need_c[ee])
            if self.waited[e].get(sem.num, 0) < v:
                waits.append((sem, v))
                self.waited[e][sem.num] = v
        for sn in sorted(need_s):
            sem, v = need_s[sn]
            if self.waited[e].get(sn, 0) < v:
                waits.append((sem, v))
                self.waited[e][sn] = v
        return waits

    def op(self, e, fn, reads=(), writes=(), tag=""):
        waits = self._gather(e, reads, writes)
        idx = len(self.streams[e])
        self.streams[e].append({"fn": fn, "waits": waits, "inc": None, "tag": tag})
        for b in writes:
            b.writers_c = {e: idx}
            b.writers_s = {}
            if not b.sink:
                b.readers_c = {}
                b.readers_s = {}
        for b in reads:
            if b.readers_c.get(e, -1) < idx:
                b.readers_c[e] = idx
        return idx

    def dma(self, e, fn, dest, reads=(), tag=""):
        waits = self._gather(e, reads, (dest,))
        if dest.sem is None:
            dest.sem = self._sem(f"dm_{dest.name}")
        dest.dma_cum += 16
        tok = (dest.sem, dest.dma_cum)
        idx = len(self.streams[e])
        self.streams[e].append(
            {"fn": fn, "waits": waits, "inc": None, "dma": tok, "tag": tag})
        if dest.sink:
            dest.writers_s[dest.sem.num] = tok
        else:
            dest.writers_c = {}
            dest.writers_s = {dest.sem.num: tok}
            dest.readers_c = {}
            dest.readers_s = {}
        for b in reads:
            b.readers_s[dest.sem.num] = tok
        return idx

    def barrier(self):
        toks = []
        for e in _ENGS:
            last = len(self.streams[e]) - 1
            while last >= 0 and self.streams[e][last]["fn"] is None:
                last -= 1
            if last < 0:
                continue
            o = self.streams[e][last]
            if o.get("dma") is not None:
                toks.append(o["dma"])
            else:
                toks.append(self._comp_token(e, last))
        seen = set()
        allsems = []
        for e in _ENGS:
            for o in self.streams[e]:
                d = o.get("dma")
                if d is not None and d[0].num not in seen:
                    seen.add(d[0].num)
        cum = {}
        for e in _ENGS:
            for o in self.streams[e]:
                d = o.get("dma")
                if d is not None:
                    sn = d[0].num
                    if sn not in cum or cum[sn][1] < d[1]:
                        cum[sn] = d
        toks.extend(cum.values())
        for e in _ENGS:
            for (sem, v) in toks:
                if self.waited[e].get(sem.num, 0) < v:
                    self.streams[e].append(
                        {"fn": None, "waits": [(sem, v)], "inc": None, "tag": "barrier"})
                    self.waited[e][sem.num] = v

    def emit(self, block):
        for e in _ENGS:
            ops = self.streams[e]
            if not ops:
                continue

            def body(eng, ops=ops, e=e):
                tl = self.tl_sem[e]
                for o in ops:
                    for (sem, v) in o["waits"]:
                        eng.wait_ge(sem, v)
                    if o["fn"] is None:
                        continue
                    ins = o["fn"](eng)
                    if o.get("dma") is not None:
                        ins.then_inc(o["dma"][0], 16)
                    elif o["inc"] is not None:
                        ins.then_inc(tl, 1)
            getattr(block, _BLOCKNAME[e])(body)

    def stats(self):
        return {e: len(self.streams[e]) for e in _ENGS} | {
            "sems": self.n_sems,
            "waits": sum(len(o["waits"]) for e in _ENGS for o in self.streams[e]),
        }



_MT_CACHE = {}


def _fourier_mt():
    if "mt" in _MT_CACHE:
        return _MT_CACHE["mt"]
    n = D
    t = np.arange(n, dtype=np.float64)
    M = np.zeros((n, n), np.float64)
    M[:, 0] = 1.0 / math.sqrt(n)
    f = np.arange(1, (n - 1) // 2 + 1, dtype=np.float64)
    ang = 2.0 * np.pi * np.outer(t, f) / n
    M[:, 1:n - 1:2] = math.sqrt(2.0 / n) * np.cos(ang)
    M[:, 2:n - 1:2] = math.sqrt(2.0 / n) * np.sin(ang)
    M[:, n - 1] = ((-1.0) ** np.arange(n)) / math.sqrt(n)
    mt = np.ascontiguousarray(M.T).astype(bfdt)
    _MT_CACHE["mt"] = mt
    return mt


def _btile(b):
    F = b.shape[0]
    return np.ascontiguousarray(b.reshape(F // 128, 128).T).astype(np.float32)


def _btile64dup(b):
    return np.ascontiguousarray(np.concatenate([b, b]).reshape(1, 128).T).astype(np.float32)



def build_program(debug_outputs=()):
    nc = bass.Bass("TRN2", target_bir_lowering=False)

    def dt_(name, shape, dtype, kind="ExternalInput"):
        return nc.dram_tensor(name, shape, dtype, kind=kind).ap()

    xt = dt_("xt", [D, S], F32)
    wdq = dt_("wdq", [D, DC1], F32)
    wdkv = dt_("wdkv", [D, DC], F32)
    wrk2 = dt_("wrk2", [D, 128], F32)
    wuq = dt_("wuq", [DC1, D], F32)
    wrq2 = dt_("wrq2", [DC1, D], F32)
    wuk = dt_("wuk", [DC, D], F32)
    wuv = dt_("wuv", [DC, D], F32)
    wout = dt_("wout", [D, D], F32)
    wbt = dt_("wbt", [D, D], F32)
    enct = dt_("enct", [D, 40], F32)
    co0 = dt_("co0", [D, 96], F32)
    co1 = dt_("co1", [D, 24], F32)
    mk0 = dt_("mk0", [D, 96], U8)
    mk1 = dt_("mk1", [D, 24], U8)
    mt = dt_("mt", [D, D], BF16)
    bdq = dt_("bdq", [128, DC1 // 128], F32)
    bdkv = dt_("bdkv", [128, DC // 128], F32)
    brk2 = dt_("brk2", [128, 1], F32)
    buq = dt_("buq", [128, NDT], F32)
    brq2 = dt_("brq2", [128, NDT], F32)
    buk = dt_("buk", [128, NDT], F32)
    buv = dt_("buv", [128, NDT], F32)
    bout = dt_("bout", [128, NDT], F32)
    bbase = dt_("bbase", [128, NDT], F32)
    identc = dt_("identc", [128, 128], BF16)
    onesc = dt_("onesc", [128, 128], BF16)

    def scratch(name, shape, dtype):
        kind = "ExternalOutput" if name in debug_outputs else "Internal"
        return dt_(name, shape, dtype, kind=kind)

    xnb = scratch("xnb", [D, S], BF16)
    qcm = scratch("qcm", [D, QOWN], BF16)
    qcr = scratch("qcr", [D, QOWN], BF16)
    oh = scratch("oh", [D, QOWN], BF16)
    vtk = scratch("vtk", [H, S, DH], BF16)
    zrows = scratch("zrows", [H, QOWN], BF16)
    dbgp = scratch("dbgp", [128, 1024], BF16)
    dbgk = scratch("dbgk", [128, 512], BF16)
    dbgr = scratch("dbgr", [128, 512], BF16)
    oat = scratch("oat", [D, QOWN], F32)
    out = dt_("out", [D, QOWN], F32, kind="ExternalOutput")

    ctx = ExitStack()
    Sc = Sched(nc, ctx)

    dB = {n: Buf(n, sink=True) for n in
          ["xnb", "qcm", "qcr", "oh", "oat", "out", "vtk", "zrows", "dbgp", "dbgk", "dbgr"]}

    with ctx:
        sb_big = {}

        def sbuf(name, shape, dtype):
            t = ctx.enter_context(nc.sbuf_tensor(name, shape, dtype))
            sb_big[name] = t
            return t, Buf(name)

        ident_t, ident_b = sbuf("identS", [128, 128], BF16)
        ones_t, ones_b = sbuf("onesS", [128, 128], BF16)
        biases = {}
        for nm, src, w in [("bdq", bdq, DC1 // 128), ("bdkv", bdkv, DC // 128),
                           ("brk2", brk2, 1), ("buk", buk, NDT), ("buv", buv, NDT),
                           ("bout", bout, NDT), ("bbase", bbase, NDT),
                           ("buq", buq, NDT), ("brq2", brq2, NDT)]:
            t, bb = sbuf("b_" + nm, [128, w], F32)
            biases[nm] = (t, bb, src)
        sbuq_t, sbuq_b = sbuf("sbuq", [128, NDT], F32)
        sbrq_t, sbrq_b = sbuf("sbrq", [128, NDT], F32)
        enc_t, enc_b = sbuf("encS", [128, NDT, 40], BF16)
        ws_t, ws_b = sbuf("wsum", [128, NDT, 40], BF16)
        gt_t, gt_b = sbuf("gtS", [40, D], BF16)

        for nm in biases:
            t, bb, src = biases[nm]
            Sc.dma("sp", lambda e, t=t, src=src: e.dma_start(out=t[:, :], in_=src[:, :]), bb)
        Sc.dma("sp", lambda e: e.dma_start(out=ident_t[:, :], in_=identc[:, :]), ident_b)
        Sc.dma("sp", lambda e: e.dma_start(out=ones_t[:, :], in_=onesc[:, :]), ones_b)
        bt, bb_, _ = biases["buq"]
        Sc.op("dve", lambda e: e.tensor_scalar_mul(sbuq_t[:, :], bt[:, :], SCALER),
              reads=[bb_], writes=[sbuq_b])
        bt2, bb2, _ = biases["brq2"]
        Sc.op("dve", lambda e: e.tensor_scalar_mul(sbrq_t[:, :], bt2[:, :], SCALER),
              reads=[bb2], writes=[sbrq_b])
        Sc.dma("gp", lambda e: e.dma_start(
            out=enc_t[:, :, :],
            in_=enct.rearrange("(t p) r -> p t r", p=128)), enc_b)

        with ExitStack() as p0:
            cof_t = p0.enter_context(nc.sbuf_tensor("cof", [128, NDT, 120], F32))
            msk_t = p0.enter_context(nc.sbuf_tensor("msk", [128, NDT, 120], F32))
            cof_bb, msk_bb = Buf("cof"), Buf("msk")
            Sc.dma("sp", lambda e: e.dma_start(
                out=cof_t[:, :, 0:96], in_=co0.rearrange("(t p) r -> p t r", p=128)), cof_bb)
            Sc.dma("sp", lambda e: e.dma_start(
                out=cof_t[:, :, 96:120], in_=co1.rearrange("(t p) r -> p t r", p=128)), cof_bb)
            Sc.dma("gp", lambda e: e.dma_start(
                out=msk_t[:, :, 0:96], in_=mk0.rearrange("(t p) r -> p t r", p=128)), msk_bb)
            Sc.dma("gp", lambda e: e.dma_start(
                out=msk_t[:, :, 96:120], in_=mk1.rearrange("(t p) r -> p t r", p=128)), msk_bb)
            prod_t = p0.enter_context(nc.sbuf_tensor("prod", [128, NDT, 120], F32))
            prod_bb = Buf("prod")
            Sc.op("dve", lambda e: e.tensor_mul(prod_t[:, :, :], cof_t[:, :, :], msk_t[:, :, :]),
                  reads=[cof_bb, msk_bb], writes=[prod_bb])
            tmp_t = p0.enter_context(nc.sbuf_tensor("tmpw", [128, NDT, 40], F32))
            tmp_bb = Buf("tmpw")
            Sc.op("dve", lambda e: e.tensor_add(tmp_t[:, :, 0:32], prod_t[:, :, 0:32],
                                                prod_t[:, :, 32:64]),
                  reads=[prod_bb], writes=[tmp_bb])
            Sc.op("dve", lambda e: e.tensor_add(tmp_t[:, :, 0:32], tmp_t[:, :, 0:32],
                                                prod_t[:, :, 64:96]),
                  reads=[prod_bb, tmp_bb], writes=[tmp_bb])
            Sc.op("dve", lambda e: e.tensor_add(tmp_t[:, :, 32:40], prod_t[:, :, 96:104],
                                                prod_t[:, :, 104:112]),
                  reads=[prod_bb, tmp_bb], writes=[tmp_bb])
            Sc.op("dve", lambda e: e.tensor_add(tmp_t[:, :, 32:40], tmp_t[:, :, 32:40],
                                                prod_t[:, :, 112:120]),
                  reads=[prod_bb, tmp_bb], writes=[tmp_bb])
            Sc.op("dve", lambda e: e.tensor_copy(ws_t[:, :, :], tmp_t[:, :, :]),
                  reads=[tmp_bb], writes=[ws_b])
            mt_ring = [p0.enter_context(nc.sbuf_tensor(f"mtr{i}", [128, 512], BF16))
                       for i in range(2)]
            mt_rb = [Buf(f"mtr{i}") for i in range(2)]
            psg = p0.enter_context(nc.psum_tensor("psg", [40, 512], F32))
            psg_b = Buf("psg")
            for jc in range(8):
                for ot in range(NDT):
                    sl = (jc * NDT + ot) % 2
                    Sc.dma("sp", lambda e, ot=ot, jc=jc, sl=sl: e.dma_start(
                        out=mt_ring[sl][:, :],
                        in_=mt[ot * 128:(ot + 1) * 128, jc * 512:(jc + 1) * 512]),
                        mt_rb[sl])
                    Sc.op("pe", lambda e, ot=ot, sl=sl: e.matmul(
                        psg[:, :], ws_t[:, ot, :], mt_ring[sl][:, :],
                        start=(ot == 0), stop=(ot == NDT - 1)),
                        reads=[ws_b, mt_rb[sl]], writes=[psg_b])
                Sc.op("act", lambda e, jc=jc: e.activation(
                    gt_t[:, jc * 512:(jc + 1) * 512], psg[:, :], AF.Identity,
                    bias=0.0, scale=0.5),
                    reads=[psg_b], writes=[gt_b])
        Sc.barrier()

        with ExitStack() as p1:
            xb_t = [p1.enter_context(nc.sbuf_tensor(f"xb{i}", [128, NDT, 512], BF16))
                    for i in range(2)]
            xb_b = [Buf(f"xb{i}") for i in range(2)]
            xn_t = [p1.enter_context(nc.sbuf_tensor(f"xn{i}", [128, NDT, 512], BF16))
                    for i in range(2)]
            xn_b = [Buf(f"xn{i}") for i in range(2)]
            sq_t = [p1.enter_context(nc.sbuf_tensor(f"sq{i}", [128, 512], BF16))
                    for i in range(4)]
            sq_b = [Buf(f"sq{i}") for i in range(4)]
            rows_t = p1.enter_context(nc.sbuf_tensor("rows", [1, 8, 512], F32))
            rows_b = Buf("rows")
            rowb_t = p1.enter_context(nc.sbuf_tensor("rowb", [1, 2, 512], BF16))
            rowb_b = Buf("rowb")
            mB_t = p1.enter_context(nc.sbuf_tensor("mB", [128, 512], BF16))
            mB_b = Buf("mB")
            rB_t = p1.enter_context(nc.sbuf_tensor("rB", [128, 512], BF16))
            rB_b = Buf("rB")
            ps_s = p1.enter_context(nc.psum_tensor("ps_s", [1, 512], F32))
            ps_q = p1.enter_context(nc.psum_tensor("ps_q", [1, 512], F32))
            ps_b1 = p1.enter_context(nc.psum_tensor("ps_b1", [128, 512], F32))
            ps_b2 = p1.enter_context(nc.psum_tensor("ps_b2", [128, 512], F32))
            ps_sb, ps_qb, ps_b1b, ps_b2b = (Buf(x) for x in
                                            ["ps_s", "ps_q", "ps_b1", "ps_b2"])
            xt_r = xt.rearrange("(t p) c -> p t c", p=128)
            for ch in range(16):
                par = ch % 2
                c0 = ch * 512
                for g in range(4):
                    Sc.dma("gp", lambda e, g=g, c0=c0, par=par: e.dma_start(
                        out=xb_t[par][:, g * 8:(g + 1) * 8, :],
                        in_=xt_r[:, g * 8:(g + 1) * 8, c0:c0 + 512]), xb_b[par],
                        tag="p1a_x")
                for dt in range(NDT):
                    sqs = (ch * NDT + dt) % 4
                    Sc.op("act", lambda e, dt=dt, par=par, sqs=sqs: e.activation(
                        sq_t[sqs][:, :], xb_t[par][:, dt, :], AF.Square),
                        reads=[xb_b[par]], writes=[sq_b[sqs]])
                    Sc.op("pe", lambda e, dt=dt, par=par: e.matmul(
                        ps_s[:, :], ones_t[:, 0:1], xb_t[par][:, dt, :],
                        start=(dt == 0), stop=(dt == NDT - 1)),
                        reads=[ones_b, xb_b[par]], writes=[ps_sb])
                    Sc.op("pe", lambda e, dt=dt, sqs=sqs: e.matmul(
                        ps_q[:, :], ones_t[:, 0:1], sq_t[sqs][:, :],
                        start=(dt == 0), stop=(dt == NDT - 1)),
                        reads=[ones_b, sq_b[sqs]], writes=[ps_qb])
                Sc.op("dve", lambda e: e.tensor_scalar_mul(rows_t[:, 0, :], ps_s[:, :], 1.0 / D),
                      reads=[ps_sb], writes=[rows_b])
                Sc.op("dve", lambda e: e.tensor_scalar_mul(rows_t[:, 1, :], ps_q[:, :], 1.0 / D),
                      reads=[ps_qb], writes=[rows_b])
                Sc.op("dve", lambda e: e.tensor_mul(rows_t[:, 2, :], rows_t[:, 0, :],
                                                    rows_t[:, 0, :]),
                      reads=[rows_b], writes=[rows_b])
                Sc.op("dve", lambda e: e.tensor_sub(rows_t[:, 3, :], rows_t[:, 1, :],
                                                    rows_t[:, 2, :]),
                      reads=[rows_b], writes=[rows_b])
                Sc.op("dve", lambda e: e.tensor_scalar_add(rows_t[:, 3, :],
                                                            rows_t[:, 3, :], EPS),
                      reads=[rows_b], writes=[rows_b])
                Sc.op("act", lambda e: e.activation(rows_t[:, 4, :], rows_t[:, 3, :],
                                                    AF.Ln),
                      reads=[rows_b], writes=[rows_b])
                Sc.op("act", lambda e: e.activation(rows_t[:, 5, :], rows_t[:, 4, :],
                                                    AF.Exp, scale=-0.5),
                      reads=[rows_b], writes=[rows_b])
                Sc.op("dve", lambda e: e.tensor_copy(rowb_t[:, 0, :], rows_t[:, 0, :]),
                      reads=[rows_b], writes=[rowb_b])
                Sc.op("dve", lambda e: e.tensor_copy(rowb_t[:, 1, :], rows_t[:, 5, :]),
                      reads=[rows_b], writes=[rowb_b])
                Sc.op("pe", lambda e: e.matmul(ps_b1[:, :], ones_t[0:1, :], rowb_t[:, 0, :],
                                               start=True, stop=True),
                      reads=[ones_b, rowb_b], writes=[ps_b1b])
                Sc.op("pe", lambda e: e.matmul(ps_b2[:, :], ones_t[0:1, :], rowb_t[:, 1, :],
                                               start=True, stop=True),
                      reads=[ones_b, rowb_b], writes=[ps_b2b])
                Sc.op("dve", lambda e: e.tensor_copy(mB_t[:, :], ps_b1[:, :]),
                      reads=[ps_b1b], writes=[mB_b])
                Sc.op("dve", lambda e: e.tensor_copy(rB_t[:, :], ps_b2[:, :]),
                      reads=[ps_b2b], writes=[rB_b])
                for dt in range(NDT):
                    Sc.op("dve", lambda e, dt=dt, par=par: e.tensor_sub(
                        xn_t[par][:, dt, :], xb_t[par][:, dt, :], mB_t[:, :]),
                        reads=[xb_b[par], mB_b], writes=[xn_b[par]])
                    Sc.op("dve", lambda e, dt=dt, par=par: e.tensor_mul(
                        xn_t[par][:, dt, :], xn_t[par][:, dt, :], rB_t[:, :]),
                        reads=[xn_b[par], rB_b], writes=[xn_b[par]])
                for g in range(4):
                    Sc.dma("sp", lambda e, g=g, c0=c0, par=par: e.dma_start(
                        out=xnb.rearrange("(t p) c -> p t c", p=128)[
                            :, g * 8:(g + 1) * 8, c0:c0 + 512],
                        in_=xn_t[par][:, g * 8:(g + 1) * 8, :]),
                        dB["xnb"], reads=[xn_b[par]], tag="p1a_st")
        Sc.barrier()

        s_cq = ExitStack()
        cq_t = s_cq.enter_context(nc.sbuf_tensor("cqS", [128, 12, QOWN], BF16))
        cq_b = Buf("cqS")
        with ExitStack() as p1c:
            wq_t = p1c.enter_context(nc.sbuf_tensor("wqS", [128, NDT, DC1], BF16))
            wq_b = Buf("wqS")
            for dt in range(NDT):
                Sc.dma("gp", lambda e, dt=dt: e.dma_start(
                    out=wq_t[:, dt, :], in_=wdq[dt * 128:(dt + 1) * 128, :]), wq_b)
            xc_t = [p1c.enter_context(nc.sbuf_tensor("xc0", [128, NDT, 512], BF16))]
            xc_b = [Buf("xc0")]
            xnb_r = xnb.rearrange("(t p) c -> p t c", p=128)
            psc = [p1c.enter_context(nc.psum_tensor(f"psc{i}", [128, 512], F32))
                   for i in range(2)]
            psc_b = [Buf(f"psc{i}") for i in range(2)]
            bt_dq = biases["bdq"]
            for ch in range(4):
                par = 0
                c0 = ch * 512
                for g in range(4):
                    Sc.dma("sp", lambda e, g=g, c0=c0, par=par: e.dma_start(
                        out=xc_t[par][:, g * 8:(g + 1) * 8, :],
                        in_=xnb_r[:, g * 8:(g + 1) * 8, c0:c0 + 512]), xc_b[par])
                for ct in range(12):
                    pp = ct % 2
                    for dt in range(NDT):
                        Sc.op("pe", lambda e, ct=ct, dt=dt, par=par, pp=pp: e.matmul(
                            psc[pp][:, :], wq_t[:, dt, ct * 128:(ct + 1) * 128],
                            xc_t[par][:, dt, :],
                            start=(dt == 0), stop=(dt == NDT - 1)),
                            reads=[wq_b, xc_b[par]], writes=[psc_b[pp]])
                    Sc.op("act", lambda e, ct=ct, c0=c0, pp=pp: e.activation(
                        cq_t[:, ct, c0:c0 + 512], psc[pp][:, :], AF.Identity,
                        bias=bt_dq[0][:, ct:ct + 1]),
                        reads=[psc_b[pp], bt_dq[1]], writes=[cq_b])
        Sc.barrier()

        with ExitStack() as p2:
            wh_t = [p2.enter_context(nc.sbuf_tensor(f"wh{i}", [128, 12, 256], BF16))
                    for i in range(2)]
            wh_b = [Buf(f"wh{i}") for i in range(2)]
            psq = [p2.enter_context(nc.psum_tensor(f"psq{i}", [128, 512], F32))
                   for i in range(4)]
            psq_b = [Buf(f"psq{i}") for i in range(4)]
            qst = [p2.enter_context(nc.sbuf_tensor(f"qst{i}", [128, QOWN], BF16))
                   for i in range(2)]
            qst_b = [Buf(f"qst{i}") for i in range(2)]
            qsr = [p2.enter_context(nc.sbuf_tensor(f"qsr{i}", [128, QOWN], BF16))
                   for i in range(2)]
            qsr_b = [Buf(f"qsr{i}") for i in range(2)]
            for h in range(H):
                par = h % 2
                Sc.dma("gp", lambda e, h=h, par=par: e.dma_start(
                    out=wh_t[par][:, :, 0:128],
                    in_=wuq.rearrange("(t p) f -> p t f", p=128)[
                        :, :, h * 128:(h + 1) * 128]), wh_b[par])
                Sc.dma("gp", lambda e, h=h, par=par: e.dma_start(
                    out=wh_t[par][:, :, 128:256],
                    in_=wrq2.rearrange("(t p) f -> p t f", p=128)[
                        :, :, h * 128:(h + 1) * 128]), wh_b[par])
                back_q = []

                def q_group(g, par=par, h=h):
                    qn, which = g // 2, g % 2
                    q0 = qn * 512
                    pp = g % 4
                    c0, c1 = (0, 128) if which == 0 else (128, 256)
                    for ct in range(12):
                        Sc.op("pe", lambda e, ct=ct, q0=q0, par=par, pp=pp,
                              c0=c0, c1=c1: e.matmul(
                            psq[pp][:, :], wh_t[par][:, ct, c0:c1],
                            cq_t[:, ct, q0:q0 + 512],
                            start=(ct == 0), stop=(ct == 11)),
                            reads=[wh_b[par], cq_b], writes=[psq_b[pp]])

                    def ev(e, h=h, q0=q0, par=par, pp=pp, which=which):
                        dst = qst if which == 0 else qsr
                        bia = sbuq_t if which == 0 else sbrq_t
                        return e.activation(
                            dst[par][:, q0:q0 + 512], psq[pp][:, :], AF.Identity,
                            bias=bia[:, h:h + 1], scale=SCALER)
                    back_q.append((ev, pp, which))

                def q_back(par=par):
                    ev, pp, which = back_q.pop(0)
                    bb = sbuq_b if which == 0 else sbrq_b
                    db = qst_b if which == 0 else qsr_b
                    Sc.op("act", ev, reads=[psq_b[pp], bb], writes=[db[par]])

                for g in range(8):
                    q_group(g)
                    if g >= 2:
                        q_back()
                q_back()
                q_back()
                Sc.dma("sp", lambda e, h=h, par=par: e.dma_start(
                    out=qcm[h * 128:(h + 1) * 128, :], in_=qst[par][:, :]),
                    dB["qcm"], reads=[qst_b[par]])
                Sc.dma("sp", lambda e, h=h, par=par: e.dma_start(
                    out=qcr[h * 128:(h + 1) * 128, :], in_=qsr[par][:, :]),
                    dB["qcr"], reads=[qsr_b[par]])
        Sc.barrier()
        s_cq.close()

        s_kv = ExitStack()
        ckv_t = s_kv.enter_context(nc.sbuf_tensor("ckvS", [128, 4, S], BF16))
        ckv_b = Buf("ckvS")
        krd_t = s_kv.enter_context(nc.sbuf_tensor("krdS", [128, S], BF16))
        krd_b = Buf("krdS")
        with ExitStack() as p1b:
            wkv_t = p1b.enter_context(nc.sbuf_tensor("wkvS", [128, NDT, DC], BF16))
            wkv_b = Buf("wkvS")
            wrk_t = p1b.enter_context(nc.sbuf_tensor("wrkS", [128, NDT, 128], BF16))
            wrk_b = Buf("wrkS")
            for dt in range(NDT):
                Sc.dma("gp", lambda e, dt=dt: e.dma_start(
                    out=wkv_t[:, dt, :], in_=wdkv[dt * 128:(dt + 1) * 128, :]), wkv_b)
                Sc.dma("gp", lambda e, dt=dt: e.dma_start(
                    out=wrk_t[:, dt, :], in_=wrk2[dt * 128:(dt + 1) * 128, :]), wrk_b)
            xk_t = [p1b.enter_context(nc.sbuf_tensor(f"xk{i}", [128, NDT, 512], BF16))
                    for i in range(2)]
            xk_b = [Buf(f"xk{i}") for i in range(2)]
            psk = [p1b.enter_context(nc.psum_tensor(f"psk{i}", [128, 512], F32))
                   for i in range(5)]
            psk_b = [Buf(f"psk{i}") for i in range(5)]
            bt_kv = biases["bdkv"]
            bt_rk = biases["brk2"]
            xnb_r2 = xnb.rearrange("(t p) c -> p t c", p=128)
            for ch in range(16):
                par = ch % 2
                c0 = ch * 512
                for g in range(4):
                    Sc.dma("sp", lambda e, g=g, c0=c0, par=par: e.dma_start(
                        out=xk_t[par][:, g * 8:(g + 1) * 8, :],
                        in_=xnb_r2[:, g * 8:(g + 1) * 8, c0:c0 + 512]), xk_b[par])
                for dt in range(NDT):
                    for ct in range(4):
                        Sc.op("pe", lambda e, ct=ct, dt=dt, par=par: e.matmul(
                            psk[ct][:, :], wkv_t[:, dt, ct * 128:(ct + 1) * 128],
                            xk_t[par][:, dt, :],
                            start=(dt == 0), stop=(dt == NDT - 1)),
                            reads=[wkv_b, xk_b[par]], writes=[psk_b[ct]])
                    Sc.op("pe", lambda e, dt=dt, par=par: e.matmul(
                        psk[4][:, :], wrk_t[:, dt, :], xk_t[par][:, dt, :],
                        start=(dt == 0), stop=(dt == NDT - 1)),
                        reads=[wrk_b, xk_b[par]], writes=[psk_b[4]])
                for ct in range(4):
                    Sc.op("act", lambda e, ct=ct, c0=c0: e.activation(
                        ckv_t[:, ct, c0:c0 + 512], psk[ct][:, :], AF.Identity,
                        bias=bt_kv[0][:, ct:ct + 1]),
                        reads=[psk_b[ct], bt_kv[1]], writes=[ckv_b])
                Sc.op("act", lambda e, c0=c0: e.activation(
                    krd_t[:, c0:c0 + 512], psk[4][:, :], AF.Identity,
                    bias=bt_rk[0][:, 0:1]),
                    reads=[psk_b[4], bt_rk[1]], writes=[krd_b])
        Sc.barrier()

        with ExitStack() as p1v:
            wv_t = [p1v.enter_context(nc.sbuf_tensor(f"wv{i}", [128, 4, 128], BF16))
                    for i in range(2)]
            wv_b = [Buf(f"wv{i}") for i in range(2)]
            vst_t = [p1v.enter_context(nc.sbuf_tensor(f"vst{i}", [128, 512], BF16))
                     for i in range(2)]
            vst_b = [Buf(f"vst{i}") for i in range(2)]
            vsg_t = [p1v.enter_context(nc.sbuf_tensor(f"vsg{i}", [128, 4, 128], BF16))
                     for i in range(2)]
            vsg_b = [Buf(f"vsg{i}") for i in range(2)]
            psA = [p1v.enter_context(nc.psum_tensor(f"psA{i}", [128, 512], F32))
                   for i in range(2)]
            psA_b = [Buf(f"psA{i}") for i in range(2)]
            psV = [p1v.enter_context(nc.psum_tensor(f"psV{i}", [128, 512], BF16))
                   for i in range(2)]
            psV_b = [Buf(f"psV{i}") for i in range(2)]
            bt_uv = biases["buv"]
            vtk_r = vtk.rearrange("h (c p) d -> h p c d", p=128)
            def v_mm(h, kc):
                par, sl, k0 = h % 2, kc % 2, kc * 512
                for ct in range(4):
                    Sc.op("pe", lambda e, ct=ct, k0=k0, par=par, sl=sl: e.matmul(
                        psA[sl][:, :], wv_t[par][:, ct, :],
                        ckv_t[:, ct, k0:k0 + 512],
                        start=(ct == 0), stop=(ct == 3)),
                        reads=[wv_b[par], ckv_b], writes=[psA_b[sl]])

            def v_back(h, kc):
                sl = kc % 2
                Sc.op("act", lambda e, h=h, sl=sl: e.activation(
                    vst_t[sl][:, :], psA[sl][:, :], AF.Identity,
                    bias=bt_uv[0][:, h:h + 1]),
                    reads=[psA_b[sl], bt_uv[1]], writes=[vst_b[sl]])
                for i in range(4):
                    Sc.op("pe", lambda e, i=i, sl=sl: e.transpose(
                        psV[sl][:, i * 128:(i + 1) * 128],
                        vst_t[sl][:, i * 128:(i + 1) * 128], ident_t[:, :]),
                        reads=[vst_b[sl], ident_b], writes=[psV_b[sl]])
                Sc.op("dve", lambda e, sl=sl: e.tensor_copy(
                    vsg_t[sl][:, :, :], psV[sl][:, :]),
                    reads=[psV_b[sl]], writes=[vsg_b[sl]])
                Sc.dma("sp", lambda e, h=h, kc=kc, sl=sl: e.dma_start(
                    out=vtk_r[h, :, kc * 4:(kc + 1) * 4, :], in_=vsg_t[sl][:, :, :]),
                    dB["vtk"], reads=[vsg_b[sl]])

            steps = [(h, kc) for h in range(H) for kc in range(16)]
            for h in range(2):
                Sc.dma("gp", lambda e, h=h: e.dma_start(
                    out=wv_t[h][:, :, :],
                    in_=wuv.rearrange("(t p) f -> p t f", p=128)[
                        :, :, h * 128:(h + 1) * 128]), wv_b[h])
            for i, (h, kc) in enumerate(steps):
                if i >= 1:
                    v_back(*steps[i - 1])
                v_mm(h, kc)
                if kc == 15 and h + 2 < H:
                    hh = h + 2
                    Sc.dma("gp", lambda e, hh=hh: e.dma_start(
                        out=wv_t[hh % 2][:, :, :],
                        in_=wuv.rearrange("(t p) f -> p t f", p=128)[
                            :, :, hh * 128:(hh + 1) * 128]), wv_b[hh % 2])
            v_back(*steps[-1])
        Sc.barrier()

        with ExitStack() as p3:
            kbuf = [p3.enter_context(nc.sbuf_tensor(f"kb{i}", [128, S], BF16))
                    for i in range(2)]
            kbuf_b = [Buf(f"kb{i}") for i in range(2)]
            vbuf = [p3.enter_context(nc.sbuf_tensor(f"vb{i}", [128, NKT, 128], BF16))
                    for i in range(2)]
            vbuf_b = [Buf(f"vb{i}") for i in range(2)]
            qm_t = [p3.enter_context(nc.sbuf_tensor(f"qm{i}", [128, QOWN], BF16))
                    for i in range(2)]
            qm_b = [Buf(f"qm{i}") for i in range(2)]
            qr_t = [p3.enter_context(nc.sbuf_tensor(f"qr{i}", [128, QOWN], BF16))
                    for i in range(2)]
            qr_b = [Buf(f"qr{i}") for i in range(2)]
            wk_t = [p3.enter_context(nc.sbuf_tensor(f"wk{i}", [128, 4, 128], BF16))
                    for i in range(2)]
            wk_b = [Buf(f"wk{i}") for i in range(2)]
            ptp_t = [p3.enter_context(nc.sbuf_tensor(f"ptp{i}", [128, 1024], BF16))
                     for i in range(3)]
            ptp_hb = [[Buf(f"ptph{i}{j}") for j in range(2)] for i in range(3)]
            zacc_t = [p3.enter_context(nc.sbuf_tensor(f"zac{i}", [128, 1024], F32))
                      for i in range(2)]
            zacc_b = [Buf(f"zac{i}") for i in range(2)]
            zacg_t = [p3.enter_context(nc.sbuf_tensor(f"zag{i}", [128, 1024], F32))
                      for i in range(2)]
            zacg_b = [Buf(f"zag{i}") for i in range(2)]
            ost_t = [p3.enter_context(nc.sbuf_tensor(f"ost{i}", [128, 1024], BF16))
                     for i in range(2)]
            ost_b = [Buf(f"ost{i}") for i in range(2)]
            zbf_t = p3.enter_context(nc.sbuf_tensor("zbf", [1, 2, 1024], BF16))
            zbf_b = [Buf(f"zbf{i}") for i in range(2)]
            psS = [p3.enter_context(nc.psum_tensor(f"psS{i}", [128, 1024], F32))
                   for i in range(2)]
            psS_b = [Buf(f"psS{i}") for i in range(2)]
            psS_hb = [[Buf(f"psSh{i}{j}") for j in range(2)] for i in range(2)]
            psZ = p3.enter_context(nc.psum_tensor("psZ", [1, 1024], F32))
            psZ_b = Buf("psZ")
            psO = p3.enter_context(nc.psum_tensor("psO", [128, 1024], F32))
            psO_b = Buf("psO")
            bt_uk = biases["buk"]
            vtk_r2 = vtk.rearrange("h (c p) d -> h p c d", p=128)

            pending_tail = [None]

            def flush_tail():
                if pending_tail[0] is not None:
                    pending_tail[0]()
                    pending_tail[0] = None

            for h in range(H):
                par = h % 2
                Sc.dma("sp", lambda e, h=h, par=par: e.dma_start(
                    out=qm_t[par][:, :], in_=qcm[h * 128:(h + 1) * 128, :]), qm_b[par])
                Sc.dma("sp", lambda e, h=h, par=par: e.dma_start(
                    out=qr_t[par][:, :], in_=qcr[h * 128:(h + 1) * 128, :]), qr_b[par])
                Sc.dma("gp", lambda e, h=h, par=par: e.dma_start(
                    out=wk_t[par][:, :, :],
                    in_=wuk.rearrange("(t p) f -> p t f", p=128)[
                        :, :, h * 128:(h + 1) * 128]), wk_b[par])
                for g in range(4):
                    Sc.dma("gp", lambda e, h=h, par=par, g=g: e.dma_start(
                        out=vbuf[par][:, g * 16:(g + 1) * 16, :],
                        in_=vtk_r2[h, :, g * 16:(g + 1) * 16, :]), vbuf_b[par],
                        reads=[dB["vtk"]])
                for kc in range(16):
                    k0 = kc * 512
                    t_, hf = kc % 2, (kc // 2) % 2
                    for ct in range(4):
                        Sc.op("pe", lambda e, ct=ct, k0=k0, par=par, t_=t_,
                              hf=hf: e.matmul(
                            psS[t_][:, hf * 512:(hf + 1) * 512], wk_t[par][:, ct, :],
                            ckv_t[:, ct, k0:k0 + 512],
                            start=(ct == 0), stop=(ct == 3)),
                            reads=[wk_b[par], ckv_b], writes=[psS_hb[t_][hf]])
                    Sc.op("act", lambda e, h=h, k0=k0, par=par, t_=t_,
                          hf=hf: e.activation(
                        kbuf[par][:, k0:k0 + 512], psS[t_][:, hf * 512:(hf + 1) * 512],
                        AF.Identity, bias=bt_uk[0][:, h:h + 1]),
                        reads=[psS_hb[t_][hf], bt_uk[1]], writes=[kbuf_b[par]])

                for qp in range(2):
                    q0 = qp * 1024
                    so = qp % 2

                    def emit_S(kt, par=par, q0=q0):
                        sl = kt % 2
                        Sc.op("pe", lambda e, kt=kt, q0=q0, par=par, sl=sl: e.matmul(
                            psS[sl][:, 0:512],
                            kbuf[par][:, kt * 128:(kt + 1) * 128],
                            qm_t[par][:, q0:q0 + 512], start=True, stop=False),
                            reads=[kbuf_b[par], qm_b[par]], writes=[psS_hb[sl][0]])
                        Sc.op("pe", lambda e, kt=kt, q0=q0, par=par, sl=sl: e.matmul(
                            psS[sl][:, 512:1024],
                            kbuf[par][:, kt * 128:(kt + 1) * 128],
                            qm_t[par][:, q0 + 512:q0 + 1024], start=True, stop=False),
                            reads=[kbuf_b[par], qm_b[par]], writes=[psS_hb[sl][1]])
                        Sc.op("pe", lambda e, kt=kt, q0=q0, par=par, sl=sl: e.matmul(
                            psS[sl][:, 0:512],
                            krd_t[0:64, kt * 128:(kt + 1) * 128],
                            qr_t[par][0:64, q0:q0 + 512], start=False, stop=True,
                            tile_position=(0, 0)),
                            reads=[krd_b, qr_b[par]], writes=[psS_hb[sl][0]])
                        Sc.op("pe", lambda e, kt=kt, q0=q0, par=par, sl=sl: e.matmul(
                            psS[sl][:, 512:1024],
                            krd_t[64:128, kt * 128:(kt + 1) * 128],
                            qr_t[par][64:128, q0 + 512:q0 + 1024], start=False,
                            stop=True, tile_position=(64, 0)),
                            reads=[krd_b, qr_b[par]], writes=[psS_hb[sl][1]])

                    def emit_back(kt, par=par, so=so):
                        s2, s3 = kt % 2, kt % 3
                        Sc.op("act", lambda e, s2=s2, s3=s3: e.activation(
                            ptp_t[s3][:, :], psS[s2][:, :], AF.Exp),
                            reads=[psS_hb[s2][0], psS_hb[s2][1]],
                            writes=[ptp_hb[s3][0], ptp_hb[s3][1]],
                            tag=f"exp_h{h}_q{q0}_kt{kt}")
                        for half in range(2):
                            Sc.op("pe", lambda e, kt=kt, s3=s3, half=half: e.matmul(
                                psO[:, half * 512:(half + 1) * 512],
                                vbuf[par][:, kt, :],
                                ptp_t[s3][:, half * 512:(half + 1) * 512],
                                start=(kt == 0), stop=(kt == NKT - 1)),
                                reads=[vbuf_b[par], ptp_hb[s3][half]], writes=[psO_b])
                        zt = zacc_t if kt % 2 == 0 else zacg_t
                        zb = zacc_b if kt % 2 == 0 else zacg_b
                        if kt < 2:
                            Sc.op("dve", lambda e, s3=s3, so=so, zt=zt:
                                  e.tensor_copy(zt[so][:, :], ptp_t[s3][:, :]),
                                  reads=[ptp_hb[s3][0], ptp_hb[s3][1]],
                                  writes=[zb[so]])
                        else:
                            zeng = "dve" if kt % 2 == 0 else "gp"
                            Sc.op(zeng, lambda e, s3=s3, so=so, zt=zt:
                                  e.tensor_add(zt[so][:, :], zt[so][:, :],
                                               ptp_t[s3][:, :]),
                                  reads=[ptp_hb[s3][0], ptp_hb[s3][1], zb[so]],
                                  writes=[zb[so]])
                        if h == 0 and q0 == 0 and kt == 0 and "dbgp" in debug_outputs:
                            Sc.dma("sp", lambda e, s3=s3: e.dma_start(
                                out=dbgp[:, :], in_=ptp_t[s3][:, :]),
                                dB["dbgp"], reads=[ptp_hb[s3][0], ptp_hb[s3][1]])
                            Sc.dma("sp", lambda e, par=par: e.dma_start(
                                out=dbgk[:, :], in_=kbuf[par][:, 0:512]),
                                dB["dbgk"], reads=[kbuf_b[par]])
                            Sc.dma("sp", lambda e: e.dma_start(
                                out=dbgr[:, :], in_=krd_t[:, 0:512]),
                                dB["dbgr"], reads=[krd_b])

                    LAG = 2
                    for kt in range(NKT):
                        if kt >= LAG:
                            emit_back(kt - LAG)
                        emit_S(kt)
                        if kt == 1:
                            flush_tail()
                    for kt in range(NKT - LAG, NKT):
                        emit_back(kt)
                    Sc.op("dve", lambda e, so=so: e.tensor_copy(
                        ost_t[so][:, :], psO[:, :]),
                        reads=[psO_b], writes=[ost_b[so]])
                    Sc.op("dve", lambda e, so=so: e.tensor_add(
                        ost_t[1 - so][:, :], zacc_t[so][:, :], zacg_t[so][:, :]),
                        reads=[zacc_b[so], zacg_b[so]], writes=[ost_b[1 - so]])
                    for half in range(2):
                        Sc.op("pe", lambda e, half=half, so=so: e.matmul(
                            psZ[:, half * 512:(half + 1) * 512], ones_t[:, 0:1],
                            ost_t[1 - so][:, half * 512:(half + 1) * 512],
                            start=True, stop=True),
                            reads=[ones_b, ost_b[1 - so]], writes=[psZ_b])
                    def _zc(e, so=so):
                        with nc.allow_low_precision(reason="Z is stored bf16"):
                            return e.tensor_copy(zbf_t[:, so, :], psZ[:, :])
                    Sc.op("dve", _zc, reads=[psZ_b], writes=[zbf_b[so]])

                    def tail_b(h=h, q0=q0, so=so):
                        Sc.dma("sp", lambda e, h=h, q0=q0, so=so: e.dma_start(
                            out=zrows[h, q0:q0 + 1024], in_=zbf_t[0:1, so, :]),
                            dB["zrows"], reads=[zbf_b[so]])
                        Sc.dma("sp", lambda e, h=h, q0=q0, so=so: e.dma_start(
                            out=oh[h * 128:(h + 1) * 128, q0:q0 + 1024],
                            in_=ost_t[so][:, :]),
                            dB["oh"], reads=[ost_b[so]])
                    pending_tail[0] = tail_b
            flush_tail()

        Sc.barrier()
        s_kv.close()

        with ExitStack() as p4:
            ohr_t = p4.enter_context(nc.sbuf_tensor("ohr", [128, NDT, QOWN], BF16))
            ohr_b = Buf("ohr")
            zr_t = p4.enter_context(nc.sbuf_tensor("zrS", [1, 2, QOWN], BF16))
            zr_b = [Buf(f"zrS{i}") for i in range(2)]
            zbc_t = [p4.enter_context(nc.sbuf_tensor(f"zbc{i}", [128, 512], F32))
                     for i in range(2)]
            zbc_b = [Buf(f"zbc{i}") for i in range(2)]
            psB4 = p4.enter_context(nc.psum_tensor("psB4", [128, 512], F32))
            psB4_b = Buf("psB4")
            oh_r = oh.rearrange("(t p) c -> p t c", p=128)
            for g in range(8):
                Sc.dma("sp", lambda e, g=g: e.dma_start(
                    out=ohr_t[:, g * 4:(g + 1) * 4, :],
                    in_=oh_r[:, g * 4:(g + 1) * 4, :]), ohr_b, reads=[dB["oh"]])
            for dt in range(NDT):
                zs = dt % 2
                Sc.dma("sp", lambda e, dt=dt, zs=zs: e.dma_start(
                    out=zr_t[:, zs, :], in_=zrows[dt, :]), zr_b[zs],
                    reads=[dB["zrows"]])
                for qn in range(4):
                    zp = qn % 2
                    Sc.op("pe", lambda e, dt=dt, qn=qn, zs=zs: e.matmul(
                        psB4[:, :], ones_t[0:1, :],
                        zr_t[0:1, zs, qn * 512:(qn + 1) * 512], start=True, stop=True),
                        reads=[ones_b, zr_b[zs]], writes=[psB4_b])
                    Sc.op("dve", lambda e, zp=zp: e.reciprocal(
                        zbc_t[zp][:, :], psB4[:, :]),
                        reads=[psB4_b], writes=[zbc_b[zp]])
                    Sc.op("dve", lambda e, dt=dt, qn=qn, zp=zp: e.tensor_mul(
                        ohr_t[:, dt, qn * 512:(qn + 1) * 512],
                        ohr_t[:, dt, qn * 512:(qn + 1) * 512], zbc_t[zp][:, :]),
                        reads=[ohr_b, zbc_b[zp]], writes=[ohr_b])
            wo_t = [p4.enter_context(nc.sbuf_tensor(f"wo{i}", [128, NDT, 128], BF16))
                    for i in range(2)]
            wo_b = [Buf(f"wo{i}") for i in range(2)]
            psP = [p4.enter_context(nc.psum_tensor(f"psP{i}", [128, 512], F32))
                   for i in range(7)]
            psP_b = [Buf(f"psP{i}") for i in range(7)]
            oa_t = [p4.enter_context(nc.sbuf_tensor(f"oaS{i}", [128, 512], F32))
                    for i in range(4)]
            oa_b = [Buf(f"oaS{i}") for i in range(4)]
            bt_o = biases["bout"]
            for ft in range(NDT):
                par = ft % 2
                for g in range(2):
                    Sc.dma("gp", lambda e, ft=ft, g=g, par=par: e.dma_start(
                        out=wo_t[par][:, g * 16:(g + 1) * 16, :],
                        in_=wout.rearrange("(t p) f -> p t f", p=128)[
                            :, g * 16:(g + 1) * 16, ft * 128:(ft + 1) * 128]),
                        wo_b[par])
                for qn in range(4):
                    pp = (ft * 4 + qn) % 7
                    q0 = qn * 512
                    for dt in range(NDT):
                        Sc.op("pe", lambda e, dt=dt, q0=q0, par=par, pp=pp: e.matmul(
                            psP[pp][:, :], wo_t[par][:, dt, :],
                            ohr_t[:, dt, q0:q0 + 512],
                            start=(dt == 0), stop=(dt == NDT - 1)),
                            reads=[wo_b[par], ohr_b], writes=[psP_b[pp]])
                    so = qn % 4
                    Sc.op("act", lambda e, ft=ft, pp=pp, so=so: e.activation(
                        oa_t[so][:, :], psP[pp][:, :], AF.Identity,
                        bias=bt_o[0][:, ft:ft + 1]),
                        reads=[psP_b[pp], bt_o[1]], writes=[oa_b[so]])
                    Sc.dma("sp", lambda e, ft=ft, q0=q0, so=so: e.dma_start(
                        out=oat[ft * 128:(ft + 1) * 128, q0:q0 + 512], in_=oa_t[so][:, :]),
                        dB["oat"], reads=[oa_b[so]])
        Sc.barrier()

        x1s = dt_("x1s", [D, QOWN], F32, kind="Internal")
        dB["x1s"] = Buf("x1s", sink=True)
        ys = dt_("ys", [D, QOWN], BF16, kind="Internal")
        dB["ys"] = Buf("ys", sink=True)
        xns = dt_("xns", [D, QOWN], BF16, kind="Internal")
        dB["xns"] = Buf("xns", sink=True)
        wbc = dt_("wbc", [D, D], BF16, kind="Internal")
        dB["wbc"] = Buf("wbc", sink=True)
        with ExitStack() as p5:
            eh_t = p5.enter_context(nc.sbuf_tensor("ehS", [40, QOWN], BF16))
            eh_b = Buf("ehS")
            r5_t = p5.enter_context(nc.sbuf_tensor("r5", [1, 8, 512], F32))
            r5_b = Buf("r5")
            r5b_t = p5.enter_context(nc.sbuf_tensor("r5b", [1, 2, 512], BF16))
            r5b_b = Buf("r5b")
            mB5_t = p5.enter_context(nc.sbuf_tensor("mB5", [128, 4, 512], F32))
            mB5_b = Buf("mB5")
            rB5_t = p5.enter_context(nc.sbuf_tensor("rB5", [128, 4, 512], F32))
            rB5_b = Buf("rB5")
            st4_t = p5.enter_context(nc.sbuf_tensor("st4", [1, 2, 4, 512], F32))
            st4_b = Buf("st4")
            psB5 = p5.enter_context(nc.psum_tensor("psB5", [128, 512], F32))
            psB5_b = Buf("psB5")
            bt_b = biases["bbase"]
            oat_r = oat.rearrange("(t p) c -> p t c", p=128)
            xt_r5 = xt.rearrange("(t p) c -> p t c", p=128)
            x1s_r = x1s.rearrange("(t p) c -> p t c", p=128)
            xns_r = xns.rearrange("(t p) c -> p t c", p=128)
            ys_r = ys.rearrange("(t p) c -> p t c", p=128)
            out_r = out.rearrange("(t p) c -> p t c", p=128)
            wbt_r = wbt.rearrange("(t p) f -> p t f", p=128)
            wbc_r = wbc.rearrange("(t p) f -> p t f", p=128)

            def bcast_rows(msl, rsl):
                Sc.op("dve", lambda e: e.tensor_copy(r5b_t[:, 0, :], r5_t[:, msl, :]),
                      reads=[r5_b], writes=[r5b_b])
                Sc.op("dve", lambda e: e.tensor_copy(r5b_t[:, 1, :], r5_t[:, rsl, :]),
                      reads=[r5_b], writes=[r5b_b])
                for j in range(4):
                    for i, (dst, dstb) in enumerate(((mB5_t, mB5_b), (rB5_t, rB5_b))):
                        Sc.op("pe", lambda e, i=i: e.matmul(
                            psB5[:, :], ones_t[0:1, :], r5b_t[:, i, :],
                            start=True, stop=True),
                            reads=[ones_b, r5b_b], writes=[psB5_b])
                        Sc.op("dve", lambda e, dst=dst, j=j: e.tensor_copy(
                            dst[:, j, :], psB5[:, :]),
                            reads=[psB5_b], writes=[dstb])

            def rows_from(sum_reads, mk_sum0, mk_sum1):
                Sc.op("dve", mk_sum0, reads=sum_reads, writes=[r5_b])
                Sc.op("dve", mk_sum1, reads=sum_reads, writes=[r5_b])
                Sc.op("dve", lambda e: e.tensor_scalar_mul(r5_t[:, 0, :],
                                                           r5_t[:, 6, :], 1.0 / D),
                      reads=[r5_b], writes=[r5_b])
                Sc.op("dve", lambda e: e.tensor_scalar_mul(r5_t[:, 1, :],
                                                           r5_t[:, 7, :], 1.0 / D),
                      reads=[r5_b], writes=[r5_b])
                Sc.op("dve", lambda e: e.tensor_mul(r5_t[:, 2, :], r5_t[:, 0, :],
                                                    r5_t[:, 0, :]),
                      reads=[r5_b], writes=[r5_b])
                Sc.op("dve", lambda e: e.tensor_sub(r5_t[:, 3, :], r5_t[:, 1, :],
                                                    r5_t[:, 2, :]),
                      reads=[r5_b], writes=[r5_b])
                Sc.op("dve", lambda e: e.tensor_scalar_add(r5_t[:, 3, :],
                                                           r5_t[:, 3, :], EPS),
                      reads=[r5_b], writes=[r5_b])
                Sc.op("act", lambda e: e.activation(r5_t[:, 4, :], r5_t[:, 3, :], AF.Ln),
                      reads=[r5_b], writes=[r5_b])
                Sc.op("act", lambda e: e.activation(r5_t[:, 5, :], r5_t[:, 4, :],
                                                    AF.Exp, scale=-0.5),
                      reads=[r5_b], writes=[r5_b])
                bcast_rows(0, 5)

            with ExitStack() as p5a:
                psL = [p5a.enter_context(nc.psum_tensor(f"psL{i}", [1, 512], F32))
                       for i in range(2)]
                psL_b = [Buf(f"psL{i}") for i in range(2)]
                psE = p5a.enter_context(nc.psum_tensor("psE", [40, 512], F32))
                psE_b = Buf("psE")
                oaCh_t = p5a.enter_context(nc.sbuf_tensor("oaCh", [128, NDT, 512], BF16))
                oaCh_b = Buf("oaCh")
                sqw_t = [p5a.enter_context(nc.sbuf_tensor(f"sqw{i}", [128, 4, 512], BF16))
                         for i in range(2)]
                sqw_b = [Buf(f"sqw{i}") for i in range(2)]
                bfw_t = [p5a.enter_context(nc.sbuf_tensor(f"bfw{i}", [128, 4, 512], BF16))
                         for i in range(2)]
                bfw_b = [Buf(f"bfw{i}") for i in range(2)]
                xrw_t = [p5a.enter_context(nc.sbuf_tensor(f"xrw{i}", [128, 4, 512], F32))
                         for i in range(2)]
                xrw_b = [Buf(f"xrw{i}") for i in range(2)]
                x1w_t = [p5a.enter_context(nc.sbuf_tensor(f"x1w{i}", [128, 4, 512], F32))
                         for i in range(2)]
                x1w_b = [Buf(f"x1w{i}") for i in range(2)]
                for ch in range(4):
                    c0 = ch * 512
                    for g in range(4):
                        Sc.dma("gp", lambda e, g=g, c0=c0: e.dma_start(
                            out=oaCh_t[:, g * 8:(g + 1) * 8, :],
                            in_=oat_r[:, g * 8:(g + 1) * 8, c0:c0 + 512]), oaCh_b,
                            reads=[dB["oat"]])
                    for g in range(8):
                        gp_ = g % 2
                        Sc.op("act", lambda e, g=g, gp_=gp_: e.activation(
                            sqw_t[gp_][:, :, :], oaCh_t[:, g * 4:(g + 1) * 4, :],
                            AF.Square),
                            reads=[oaCh_b], writes=[sqw_b[gp_]])
                        for j in range(4):
                            dt = g * 4 + j
                            Sc.op("pe", lambda e, dt=dt, j=j, g=g: e.matmul(
                                psL[0][:, :], ones_t[:, 0:1],
                                oaCh_t[:, g * 4 + j, :],
                                start=(dt == 0), stop=(dt == NDT - 1)),
                                reads=[ones_b, oaCh_b], writes=[psL_b[0]])
                            Sc.op("pe", lambda e, dt=dt, j=j, gp_=gp_: e.matmul(
                                psL[1][:, :], ones_t[:, 0:1], sqw_t[gp_][:, j, :],
                                start=(dt == 0), stop=(dt == NDT - 1)),
                                reads=[ones_b, sqw_b[gp_]], writes=[psL_b[1]])
                    rows_from([psL_b[0], psL_b[1]],
                              lambda e: e.tensor_copy(r5_t[:, 6, :], psL[0][:, :]),
                              lambda e: e.tensor_copy(r5_t[:, 7, :], psL[1][:, :]))
                    for g in range(8):
                        gp_ = g % 2
                        Sc.dma("sp", lambda e, g=g, c0=c0, gp_=gp_: e.dma_start(
                            out=xrw_t[gp_][:, :, :],
                            in_=xt_r5[:, g * 4:(g + 1) * 4, c0:c0 + 512]), xrw_b[gp_])
                        Sc.op("dve", lambda e, g=g, gp_=gp_: e.tensor_sub(
                            x1w_t[gp_][:, :, :], oaCh_t[:, g * 4:(g + 1) * 4, :],
                            mB5_t[:, :, :]),
                            reads=[oaCh_b, mB5_b], writes=[x1w_b[gp_]])
                        Sc.op("dve", lambda e, gp_=gp_: e.tensor_mul(
                            x1w_t[gp_][:, :, :], x1w_t[gp_][:, :, :], rB5_t[:, :, :]),
                            reads=[x1w_b[gp_], rB5_b], writes=[x1w_b[gp_]])
                        Sc.op("dve", lambda e, gp_=gp_: e.tensor_add(
                            x1w_t[gp_][:, :, :], x1w_t[gp_][:, :, :], xrw_t[gp_][:, :, :]),
                            reads=[x1w_b[gp_], xrw_b[gp_]], writes=[x1w_b[gp_]])
                        Sc.dma("sp", lambda e, g=g, c0=c0, gp_=gp_: e.dma_start(
                            out=x1s_r[:, g * 4:(g + 1) * 4, c0:c0 + 512],
                            in_=x1w_t[gp_][:, :, :]), dB["x1s"], reads=[x1w_b[gp_]])
                        Sc.op("dve", lambda e, gp_=gp_: e.tensor_copy(
                            bfw_t[gp_][:, :, :], x1w_t[gp_][:, :, :]),
                            reads=[x1w_b[gp_]], writes=[bfw_b[gp_]])
                        Sc.op("act", lambda e, gp_=gp_: e.activation(
                            sqw_t[gp_][:, :, :], bfw_t[gp_][:, :, :], AF.Square),
                            reads=[bfw_b[gp_]], writes=[sqw_b[gp_]])
                        for j in range(4):
                            dt = g * 4 + j
                            Sc.op("pe", lambda e, dt=dt, j=j, gp_=gp_: e.matmul(
                                psL[0][:, :], ones_t[:, 0:1], bfw_t[gp_][:, j, :],
                                start=(dt == 0), stop=(dt == NDT - 1)),
                                reads=[ones_b, bfw_b[gp_]], writes=[psL_b[0]])
                            Sc.op("pe", lambda e, dt=dt, j=j, gp_=gp_: e.matmul(
                                psL[1][:, :], ones_t[:, 0:1], sqw_t[gp_][:, j, :],
                                start=(dt == 0), stop=(dt == NDT - 1)),
                                reads=[ones_b, sqw_b[gp_]], writes=[psL_b[1]])
                    rows_from([psL_b[0], psL_b[1]],
                              lambda e: e.tensor_copy(r5_t[:, 6, :], psL[0][:, :]),
                              lambda e: e.tensor_copy(r5_t[:, 7, :], psL[1][:, :]))
                    for g in range(8):
                        gp_ = g % 2
                        Sc.dma("sp", lambda e, g=g, c0=c0, gp_=gp_: e.dma_start(
                            out=x1w_t[gp_][:, :, :],
                            in_=x1s_r[:, g * 4:(g + 1) * 4, c0:c0 + 512]), x1w_b[gp_],
                            reads=[dB["x1s"]])
                        Sc.op("dve", lambda e, gp_=gp_: e.tensor_sub(
                            xrw_t[gp_][:, :, :], x1w_t[gp_][:, :, :], mB5_t[:, :, :]),
                            reads=[x1w_b[gp_], mB5_b], writes=[xrw_b[gp_]])
                        Sc.op("dve", lambda e, gp_=gp_: e.tensor_mul(
                            bfw_t[gp_][:, :, :], xrw_t[gp_][:, :, :], rB5_t[:, :, :]),
                            reads=[xrw_b[gp_], rB5_b], writes=[bfw_b[gp_]])
                        Sc.dma("sp", lambda e, g=g, c0=c0, gp_=gp_: e.dma_start(
                            out=xns_r[:, g * 4:(g + 1) * 4, c0:c0 + 512],
                            in_=bfw_t[gp_][:, :, :]), dB["xns"], reads=[bfw_b[gp_]])
                        for j in range(4):
                            dt = g * 4 + j
                            Sc.op("pe", lambda e, dt=dt, j=j, gp_=gp_: e.matmul(
                                psE[:, :], enc_t[:, dt, :], bfw_t[gp_][:, j, :],
                                start=(dt == 0), stop=(dt == NDT - 1)),
                                reads=[enc_b, bfw_b[gp_]], writes=[psE_b])
                    Sc.op("dve", lambda e, c0=c0: e.tensor_copy(
                        eh_t[:, c0:c0 + 512], psE[:, :]),
                        reads=[psE_b], writes=[eh_b])
            Sc.barrier()

            with ExitStack() as p5b:
                xnh_t = p5b.enter_context(nc.sbuf_tensor("xnh", [128, NDT, 1024], BF16))
                xnh_b = Buf("xnh")
                wb5_t = [p5b.enter_context(nc.sbuf_tensor(f"wb5{i}", [128, NDT, 128],
                                                          BF16)) for i in range(2)]
                wb5_b = [Buf(f"wb5{i}") for i in range(2)]
                psY = [p5b.enter_context(nc.psum_tensor(f"psY{i}", [128, 512], F32))
                       for i in range(4)]
                psY_b = [Buf(f"psY{i}") for i in range(4)]
                psT = [p5b.enter_context(nc.psum_tensor(f"psT{i}", [1, 512], F32))
                       for i in range(2)]
                psT_b = [Buf(f"psT{i}") for i in range(2)]
                sq2_t = [p5b.enter_context(nc.sbuf_tensor(f"sq2{i}", [128, 512], BF16))
                         for i in range(2)]
                sq2_b = [Buf(f"sq2{i}") for i in range(2)]
                yst_t = [p5b.enter_context(nc.sbuf_tensor(f"yst{i}", [128, 512], BF16))
                         for i in range(4)]
                yst_b = [Buf(f"yst{i}") for i in range(4)]
                wst_t = [p5b.enter_context(nc.sbuf_tensor(f"wst{i}", [128, 16, 128],
                                                          BF16)) for i in range(2)]
                wst_b = [Buf(f"wst{i}") for i in range(2)]
                Sc.op("dve", lambda e: e.memset(st4_t[:, :, :, :], 0.0),
                      writes=[st4_b])
                back_y = []

                def y_group(hf, ot, qn):
                    par, pp = ot % 2, (ot * 2 + qn) % 4
                    q0 = hf * 1024 + qn * 512
                    ql = qn * 512
                    for dt in range(NDT):
                        Sc.op("pe", lambda e, dt=dt, ql=ql, par=par, pp=pp: e.matmul(
                            psY[pp][:, :], wb5_t[par][:, dt, :],
                            xnh_t[:, dt, ql:ql + 512],
                            start=(dt == 0), stop=False),
                            reads=[wb5_b[par], xnh_b], writes=[psY_b[pp]])
                    Sc.op("pe", lambda e, ot=ot, q0=q0, pp=pp: e.matmul(
                        psY[pp][:, :], gt_t[0:40, ot * 128:(ot + 1) * 128],
                        eh_t[:, q0:q0 + 512], start=False, stop=True),
                        reads=[gt_b, eh_b], writes=[psY_b[pp]])
                    back_y.append((hf, ot, qn, pp))

                def y_back():
                    hf, ot, qn, pp = back_y.pop(0)
                    q0 = hf * 1024 + qn * 512
                    sp_ = (ot * 2 + qn) % 2
                    ysl = (ot * 2 + qn) % 4
                    qn4 = hf * 2 + qn
                    Sc.op("act", lambda e, pp=pp, ot=ot, ysl=ysl: e.activation(
                        yst_t[ysl][:, :], psY[pp][:, :], AF.Identity,
                        bias=bt_b[0][:, ot:ot + 1]),
                        reads=[psY_b[pp], bt_b[1]], writes=[yst_b[ysl]])
                    Sc.dma("sp", lambda e, ot=ot, q0=q0, ysl=ysl: e.dma_start(
                        out=ys_r[:, ot, q0:q0 + 512], in_=yst_t[ysl][:, :]),
                        dB["ys"], reads=[yst_b[ysl]])
                    Sc.op("act", lambda e, sp_=sp_, ysl=ysl: e.activation(
                        sq2_t[sp_][:, :], yst_t[ysl][:, :], AF.Square),
                        reads=[yst_b[ysl]], writes=[sq2_b[sp_]])
                    Sc.op("pe", lambda e, ysl=ysl: e.matmul(
                        psT[0][:, :], ones_t[:, 0:1], yst_t[ysl][:, :],
                        start=True, stop=True),
                        reads=[ones_b, yst_b[ysl]], writes=[psT_b[0]])
                    Sc.op("pe", lambda e, sp_=sp_: e.matmul(
                        psT[1][:, :], ones_t[:, 0:1], sq2_t[sp_][:, :],
                        start=True, stop=True),
                        reads=[ones_b, sq2_b[sp_]], writes=[psT_b[1]])
                    Sc.op("dve", lambda e, qn4=qn4: e.tensor_add(
                        st4_t[:, 0, qn4, :], st4_t[:, 0, qn4, :], psT[0][:, :]),
                        reads=[psT_b[0], st4_b], writes=[st4_b])
                    Sc.op("dve", lambda e, qn4=qn4: e.tensor_add(
                        st4_t[:, 1, qn4, :], st4_t[:, 1, qn4, :], psT[1][:, :]),
                        reads=[psT_b[1], st4_b], writes=[st4_b])

                for hf in range(2):
                    for g in range(4):
                        Sc.dma("sp", lambda e, g=g, hf=hf: e.dma_start(
                            out=xnh_t[:, g * 8:(g + 1) * 8, :],
                            in_=xns_r[:, g * 8:(g + 1) * 8,
                                      hf * 1024:(hf + 1) * 1024]), xnh_b,
                            reads=[dB["xns"]])
                    for par in range(2):
                        for g in range(2):
                            if hf == 0:
                                Sc.dma("gp", lambda e, par=par, g=g: e.dma_start(
                                    out=wb5_t[par][:, g * 16:(g + 1) * 16, :],
                                    in_=wbt_r[:, g * 16:(g + 1) * 16,
                                              par * 128:(par + 1) * 128]), wb5_b[par])
                            else:
                                Sc.dma("sp", lambda e, par=par, g=g: e.dma_start(
                                    out=wb5_t[par][:, g * 16:(g + 1) * 16, :],
                                    in_=wbc_r[:, g * 16:(g + 1) * 16,
                                              par * 128:(par + 1) * 128]), wb5_b[par],
                                    reads=[dB["wbc"]])
                    for ot in range(NDT):
                        if hf == 0:
                            for g in range(2):
                                Sc.dma("sp", lambda e, ot=ot, g=g: e.dma_start(
                                    out=wbc_r[:, g * 16:(g + 1) * 16,
                                              ot * 128:(ot + 1) * 128],
                                    in_=wst_t[0][:, :, :][:, g * 16:(g + 1) * 16, :]
                                    if False else wb5_t[ot % 2][:, g * 16:(g + 1) * 16, :]),
                                    dB["wbc"], reads=[wb5_b[ot % 2]])
                        for qn in range(2):
                            if len(back_y) >= 2:
                                y_back()
                            y_group(hf, ot, qn)
                        if ot + 2 < NDT:
                            oo = ot + 2
                            for g in range(2):
                                if hf == 0:
                                    Sc.dma("gp", lambda e, oo=oo, g=g: e.dma_start(
                                        out=wb5_t[oo % 2][:, g * 16:(g + 1) * 16, :],
                                        in_=wbt_r[:, g * 16:(g + 1) * 16,
                                                  oo * 128:(oo + 1) * 128]),
                                        wb5_b[oo % 2])
                                else:
                                    Sc.dma("sp", lambda e, oo=oo, g=g: e.dma_start(
                                        out=wb5_t[oo % 2][:, g * 16:(g + 1) * 16, :],
                                        in_=wbc_r[:, g * 16:(g + 1) * 16,
                                                  oo * 128:(oo + 1) * 128]),
                                        wb5_b[oo % 2], reads=[dB["wbc"]])
                    while back_y:
                        y_back()
            Sc.barrier()

            with ExitStack() as p5c:
                x1r_t = [p5c.enter_context(nc.sbuf_tensor(f"x1r{i}", [128, 4, 512], F32))
                         for i in range(2)]
                x1r_b = [Buf(f"x1r{i}") for i in range(2)]
                ow_t = [p5c.enter_context(nc.sbuf_tensor(f"ow{i}", [128, 4, 512], F32))
                        for i in range(2)]
                ow_b = [Buf(f"ow{i}") for i in range(2)]
                yw_t = [p5c.enter_context(nc.sbuf_tensor(f"yw{i}", [128, 4, 512], BF16))
                        for i in range(2)]
                yw_b = [Buf(f"yw{i}") for i in range(2)]
                for ch in range(4):
                    c0 = ch * 512
                    rows_from([st4_b],
                              lambda e, ch=ch: e.tensor_copy(r5_t[:, 6, :],
                                                             st4_t[:, 0, ch, :]),
                              lambda e, ch=ch: e.tensor_copy(r5_t[:, 7, :],
                                                             st4_t[:, 1, ch, :]))
                    for g in range(8):
                        gp_ = g % 2
                        Sc.dma("sp", lambda e, g=g, c0=c0, gp_=gp_: e.dma_start(
                            out=x1r_t[gp_][:, :, :],
                            in_=x1s_r[:, g * 4:(g + 1) * 4, c0:c0 + 512]), x1r_b[gp_],
                            reads=[dB["x1s"]])
                        Sc.dma("sp", lambda e, g=g, c0=c0, gp_=gp_: e.dma_start(
                            out=yw_t[gp_][:, :, :],
                            in_=ys_r[:, g * 4:(g + 1) * 4, c0:c0 + 512]), yw_b[gp_],
                            reads=[dB["ys"]])
                        Sc.op("dve", lambda e, gp_=gp_: e.tensor_sub(
                            ow_t[gp_][:, :, :], yw_t[gp_][:, :, :], mB5_t[:, :, :]),
                            reads=[yw_b[gp_], mB5_b], writes=[ow_b[gp_]])
                        Sc.op("dve", lambda e, gp_=gp_: e.tensor_mul(
                            ow_t[gp_][:, :, :], ow_t[gp_][:, :, :], rB5_t[:, :, :]),
                            reads=[ow_b[gp_], rB5_b], writes=[ow_b[gp_]])
                        Sc.op("dve", lambda e, gp_=gp_: e.tensor_add(
                            ow_t[gp_][:, :, :], ow_t[gp_][:, :, :], x1r_t[gp_][:, :, :]),
                            reads=[ow_b[gp_], x1r_b[gp_]], writes=[ow_b[gp_]])
                        Sc.dma("sp", lambda e, g=g, c0=c0, gp_=gp_: e.dma_start(
                            out=out_r[:, g * 4:(g + 1) * 4, c0:c0 + 512],
                            in_=ow_t[gp_][:, :, :]), dB["out"], reads=[ow_b[gp_]])

        Sc.op("sp", lambda e: e.nop(), reads=[dB["out"]])

        print("kernel build stats:", Sc.stats())
        with nc.Block() as block:
            Sc.emit(block)
    return nc



def _prep_shared(ins):
    a = {}
    a["wdq"] = np.ascontiguousarray(ins["L0_attn_w_dq"], np.float32)
    a["wdkv"] = np.ascontiguousarray(ins["L0_attn_w_dkv"], np.float32)
    wrk = np.asarray(ins["L0_attn_w_rk"], np.float32)
    a["wrk2"] = np.ascontiguousarray(np.concatenate([wrk, wrk], axis=1))
    a["wuq"] = np.ascontiguousarray(ins["L0_attn_w_uq"], np.float32)
    wrq = np.asarray(ins["L0_attn_w_rq"], np.float32)
    wrq2 = np.empty((DC1, D), np.float32)
    for h in range(H):
        blk = wrq[:, h * 64:(h + 1) * 64]
        wrq2[:, h * 128:h * 128 + 64] = blk
        wrq2[:, h * 128 + 64:(h + 1) * 128] = blk
    a["wrq2"] = wrq2
    a["wuk"] = np.ascontiguousarray(ins["L0_attn_w_uk"], np.float32)
    a["wuv"] = np.ascontiguousarray(ins["L0_attn_w_uv"], np.float32)
    a["wout"] = np.ascontiguousarray(ins["L0_attn_w_out"], np.float32)
    a["wbt"] = np.ascontiguousarray(np.asarray(ins["L0_ffn_W_base"], np.float32).T)
    a["enct"] = np.ascontiguousarray(np.concatenate(
        [np.asarray(ins["L0_ffn_enc0"], np.float32).T,
         np.asarray(ins["L0_ffn_enc1"], np.float32).T], axis=1))
    a["co0"] = np.ascontiguousarray(
        np.transpose(np.asarray(ins["L0_ffn_coefs0"], np.float32), (1, 0, 2)).reshape(D, 96))
    a["co1"] = np.ascontiguousarray(
        np.transpose(np.asarray(ins["L0_ffn_coefs1"], np.float32), (1, 0, 2)).reshape(D, 24))
    a["mk0"] = np.ascontiguousarray(
        np.transpose(np.asarray(ins["L0_ffn_mask0"]), (1, 0, 2)).reshape(D, 96)).astype(np.uint8)
    a["mk1"] = np.ascontiguousarray(
        np.transpose(np.asarray(ins["L0_ffn_mask1"]), (1, 0, 2)).reshape(D, 24)).astype(np.uint8)
    a["mt"] = _fourier_mt()
    a["bdq"] = _btile(np.asarray(ins["L0_attn_b_dq"], np.float32))
    a["bdkv"] = _btile(np.asarray(ins["L0_attn_b_dkv"], np.float32))
    a["brk2"] = _btile64dup(np.asarray(ins["L0_attn_b_rk"], np.float32))
    a["buq"] = _btile(np.asarray(ins["L0_attn_b_uq"], np.float32))
    brq = np.asarray(ins["L0_attn_b_rq"], np.float32).reshape(H, 64)
    a["brq2"] = np.ascontiguousarray(
        np.concatenate([brq, brq], axis=1).T)
    a["buk"] = _btile(np.asarray(ins["L0_attn_b_uk"], np.float32))
    a["buv"] = _btile(np.asarray(ins["L0_attn_b_uv"], np.float32))
    a["bout"] = _btile(np.asarray(ins["L0_attn_b_out"], np.float32))
    a["bbase"] = _btile(np.asarray(ins["L0_ffn_b_base"], np.float32))
    a["identc"] = np.eye(128, dtype=np.float32).astype(bfdt)
    a["onesc"] = np.ones((128, 128), np.float32).astype(bfdt)
    return a


def kernel(**ins):
    x = np.asarray(ins["L0_attn_sequence"], np.float32)
    shared = _prep_shared(ins)
    nc = build_program()
    in_maps = []
    for c in range(NCORES):
        b, qi = c // 4, c % 4
        xtb = np.ascontiguousarray(x[b].T)
        qo = qi * QOWN
        perm = np.concatenate([xtb[:, qo:qo + QOWN], xtb[:, :qo], xtb[:, qo + QOWN:]],
                              axis=1)
        m = dict(shared)
        m["xt"] = np.ascontiguousarray(perm)
        in_maps.append(m)
    res = run_bass_kernel_spmd(nc, in_maps, core_ids=list(range(NCORES)))
    outp = np.empty((B, S, D), np.float32)
    for c in range(NCORES):
        b, qi = c // 4, c % 4
        outp[b, qi * QOWN:(qi + 1) * QOWN, :] = res.results[c]["out"].T
    return outp


# version 23
# speedup vs baseline: 1.0209x; 1.0209x over previous
import math
from contextlib import ExitStack

import numpy as np
import ml_dtypes

import concourse.bass as bass
import concourse.mybir as mybir
from concourse.bass_utils import run_bass_kernel_spmd


F32 = mybir.dt.float32
BF16 = mybir.dt.bfloat16
U8 = mybir.dt.uint8
AF = mybir.ActivationFunctionType
bfdt = ml_dtypes.bfloat16

B, S, D = 2, 8192, 4096
H, DH, DR, DC, DC1 = 32, 128, 64, 512, 1536
NCORES = 8
QOWN = 2048
SCALER = 1.0 / math.sqrt(DH + DR)
EPS = 1e-5
NDT = D // 128
NKT = S // 128
SAME_ENG_SYNC = True


_ENGS = ("pe", "act", "dve", "sp", "gp")
_BLOCKNAME = {"pe": "tensor", "act": "scalar", "dve": "vector", "sp": "sync", "gp": "gpsimd"}


class Buf:

    __slots__ = ("name", "writers_c", "writers_s", "readers_c", "readers_s",
                 "sem", "dma_cum", "sink")

    def __init__(self, name, sink=False):
        self.name = name
        self.writers_c = {}
        self.writers_s = {}
        self.readers_c = {}
        self.readers_s = {}
        self.sem = None
        self.dma_cum = 0
        self.sink = sink


class Sched:
    def __init__(self, nc, ctx):
        self.nc = nc
        self.ctx = ctx
        self.streams = {e: [] for e in _ENGS}
        self.inc_idx = {e: [] for e in _ENGS}
        self.inc_val = {e: [] for e in _ENGS}
        self.tl_sem = {e: None for e in _ENGS}
        self.waited = {e: {} for e in _ENGS}
        self.n_sems = 0

    def _sem(self, name):
        h = self.ctx.enter_context(self.nc.semaphore(name))
        self.n_sems += 1
        return h

    def _tl(self, e):
        if self.tl_sem[e] is None:
            self.tl_sem[e] = self._sem(f"tl_{e}")
        return self.tl_sem[e]

    def _comp_token(self, e, idx):
        import bisect
        ii = self.inc_idx[e]
        j = bisect.bisect_left(ii, idx)
        if j < len(ii):
            return (self._tl(e), self.inc_val[e][j])
        while self.streams[e][idx]["fn"] is None:
            idx += 1
            assert idx < len(self.streams[e]), "token needed past end of stream"
        v = (self.inc_val[e][-1] if self.inc_val[e] else 0) + 1
        self.streams[e][idx]["inc"] = v
        ii.append(idx)
        self.inc_val[e].append(v)
        return (self._tl(e), v)

    def _gather(self, e, reads, writes):
        need_c = {}
        need_s = {}

        def addc(d):
            for ee, ii in d.items():
                if ii > need_c.get(ee, -1):
                    need_c[ee] = ii

        def adds(d):
            for sn, (sem, v) in d.items():
                if v > need_s.get(sn, (None, -1))[1]:
                    need_s[sn] = (sem, v)

        for b in reads:
            addc(b.writers_c)
            adds(b.writers_s)
        for b in writes:
            addc(b.writers_c)
            adds(b.writers_s)
            if not b.sink:
                addc(b.readers_c)
                adds(b.readers_s)
        waits = []
        for ee in sorted(need_c):
            if ee == e and (ee == "pe" or not SAME_ENG_SYNC):
                continue
            sem, v = self._comp_token(ee, need_c[ee])
            if self.waited[e].get(sem.num, 0) < v:
                waits.append((sem, v))
                self.waited[e][sem.num] = v
        for sn in sorted(need_s):
            sem, v = need_s[sn]
            if self.waited[e].get(sn, 0) < v:
                waits.append((sem, v))
                self.waited[e][sn] = v
        return waits

    def op(self, e, fn, reads=(), writes=(), tag=""):
        waits = self._gather(e, reads, writes)
        idx = len(self.streams[e])
        self.streams[e].append({"fn": fn, "waits": waits, "inc": None, "tag": tag})
        for b in writes:
            b.writers_c = {e: idx}
            b.writers_s = {}
            if not b.sink:
                b.readers_c = {}
                b.readers_s = {}
        for b in reads:
            if b.readers_c.get(e, -1) < idx:
                b.readers_c[e] = idx
        return idx

    def dma(self, e, fn, dest, reads=(), tag=""):
        waits = self._gather(e, reads, (dest,))
        if dest.sem is None:
            dest.sem = self._sem(f"dm_{dest.name}")
        dest.dma_cum += 16
        tok = (dest.sem, dest.dma_cum)
        idx = len(self.streams[e])
        self.streams[e].append(
            {"fn": fn, "waits": waits, "inc": None, "dma": tok, "tag": tag})
        if dest.sink:
            dest.writers_s[dest.sem.num] = tok
        else:
            dest.writers_c = {}
            dest.writers_s = {dest.sem.num: tok}
            dest.readers_c = {}
            dest.readers_s = {}
        for b in reads:
            b.readers_s[dest.sem.num] = tok
        return idx

    def barrier(self):
        toks = []
        for e in _ENGS:
            last = len(self.streams[e]) - 1
            while last >= 0 and self.streams[e][last]["fn"] is None:
                last -= 1
            if last < 0:
                continue
            o = self.streams[e][last]
            if o.get("dma") is not None:
                toks.append(o["dma"])
            else:
                toks.append(self._comp_token(e, last))
        seen = set()
        allsems = []
        for e in _ENGS:
            for o in self.streams[e]:
                d = o.get("dma")
                if d is not None and d[0].num not in seen:
                    seen.add(d[0].num)
        cum = {}
        for e in _ENGS:
            for o in self.streams[e]:
                d = o.get("dma")
                if d is not None:
                    sn = d[0].num
                    if sn not in cum or cum[sn][1] < d[1]:
                        cum[sn] = d
        toks.extend(cum.values())
        for e in _ENGS:
            for (sem, v) in toks:
                if self.waited[e].get(sem.num, 0) < v:
                    self.streams[e].append(
                        {"fn": None, "waits": [(sem, v)], "inc": None, "tag": "barrier"})
                    self.waited[e][sem.num] = v

    def emit(self, block):
        for e in _ENGS:
            ops = self.streams[e]
            if not ops:
                continue

            def body(eng, ops=ops, e=e):
                tl = self.tl_sem[e]
                for o in ops:
                    for (sem, v) in o["waits"]:
                        eng.wait_ge(sem, v)
                    if o["fn"] is None:
                        continue
                    ins = o["fn"](eng)
                    if o.get("dma") is not None:
                        ins.then_inc(o["dma"][0], 16)
                    elif o["inc"] is not None:
                        ins.then_inc(tl, 1)
            getattr(block, _BLOCKNAME[e])(body)

    def stats(self):
        return {e: len(self.streams[e]) for e in _ENGS} | {
            "sems": self.n_sems,
            "waits": sum(len(o["waits"]) for e in _ENGS for o in self.streams[e]),
        }



_MT_CACHE = {}


def _fourier_mt():
    if "mt" in _MT_CACHE:
        return _MT_CACHE["mt"]
    n = D
    t = np.arange(n, dtype=np.float64)
    M = np.zeros((n, n), np.float64)
    M[:, 0] = 1.0 / math.sqrt(n)
    f = np.arange(1, (n - 1) // 2 + 1, dtype=np.float64)
    ang = 2.0 * np.pi * np.outer(t, f) / n
    M[:, 1:n - 1:2] = math.sqrt(2.0 / n) * np.cos(ang)
    M[:, 2:n - 1:2] = math.sqrt(2.0 / n) * np.sin(ang)
    M[:, n - 1] = ((-1.0) ** np.arange(n)) / math.sqrt(n)
    mt = np.ascontiguousarray(M.T).astype(bfdt)
    _MT_CACHE["mt"] = mt
    return mt


def _btile(b):
    F = b.shape[0]
    return np.ascontiguousarray(b.reshape(F // 128, 128).T).astype(np.float32)


def _btile64dup(b):
    return np.ascontiguousarray(np.concatenate([b, b]).reshape(1, 128).T).astype(np.float32)



def build_program(debug_outputs=()):
    nc = bass.Bass("TRN2", target_bir_lowering=False)

    def dt_(name, shape, dtype, kind="ExternalInput"):
        return nc.dram_tensor(name, shape, dtype, kind=kind).ap()

    xt = dt_("xt", [D, S], F32)
    wdq = dt_("wdq", [D, DC1], F32)
    wdkv = dt_("wdkv", [D, DC], F32)
    wrk2 = dt_("wrk2", [D, 128], F32)
    wuq = dt_("wuq", [DC1, D], F32)
    wrq2 = dt_("wrq2", [DC1, D], F32)
    wuk = dt_("wuk", [DC, D], F32)
    wuv = dt_("wuv", [DC, D], F32)
    wout = dt_("wout", [D, D], F32)
    wbt = dt_("wbt", [D, D], F32)
    enct = dt_("enct", [D, 40], F32)
    co0 = dt_("co0", [D, 96], F32)
    co1 = dt_("co1", [D, 24], F32)
    mk0 = dt_("mk0", [D, 96], U8)
    mk1 = dt_("mk1", [D, 24], U8)
    mt = dt_("mt", [D, D], BF16)
    bdq = dt_("bdq", [128, DC1 // 128], F32)
    bdkv = dt_("bdkv", [128, DC // 128], F32)
    brk2 = dt_("brk2", [128, 1], F32)
    buq = dt_("buq", [128, NDT], F32)
    brq2 = dt_("brq2", [128, NDT], F32)
    buk = dt_("buk", [128, NDT], F32)
    buv = dt_("buv", [128, NDT], F32)
    bout = dt_("bout", [128, NDT], F32)
    bbase = dt_("bbase", [128, NDT], F32)
    identc = dt_("identc", [128, 128], BF16)
    onesc = dt_("onesc", [128, 128], BF16)

    def scratch(name, shape, dtype):
        kind = "ExternalOutput" if name in debug_outputs else "Internal"
        return dt_(name, shape, dtype, kind=kind)

    xnb = scratch("xnb", [D, S], BF16)
    qcm = scratch("qcm", [D, QOWN], BF16)
    qcr = scratch("qcr", [D, QOWN], BF16)
    oh = scratch("oh", [D, QOWN], BF16)
    vtk = scratch("vtk", [H, S, DH], BF16)
    zrows = scratch("zrows", [H, QOWN], BF16)
    dbgp = scratch("dbgp", [128, 1024], BF16)
    dbgk = scratch("dbgk", [128, 512], BF16)
    dbgr = scratch("dbgr", [128, 512], BF16)
    oat = scratch("oat", [D, QOWN], F32)
    out = dt_("out", [D, QOWN], F32, kind="ExternalOutput")

    ctx = ExitStack()
    Sc = Sched(nc, ctx)

    dB = {n: Buf(n, sink=True) for n in
          ["xnb", "qcm", "qcr", "oh", "oat", "out", "vtk", "zrows", "dbgp", "dbgk", "dbgr"]}

    with ctx:
        sb_big = {}

        def sbuf(name, shape, dtype):
            t = ctx.enter_context(nc.sbuf_tensor(name, shape, dtype))
            sb_big[name] = t
            return t, Buf(name)

        ident_t, ident_b = sbuf("identS", [128, 128], BF16)
        ones_t, ones_b = sbuf("onesS", [128, 128], BF16)
        biases = {}
        for nm, src, w in [("bdq", bdq, DC1 // 128), ("bdkv", bdkv, DC // 128),
                           ("brk2", brk2, 1), ("buk", buk, NDT), ("buv", buv, NDT),
                           ("bout", bout, NDT), ("bbase", bbase, NDT),
                           ("buq", buq, NDT), ("brq2", brq2, NDT)]:
            t, bb = sbuf("b_" + nm, [128, w], F32)
            biases[nm] = (t, bb, src)
        sbuq_t, sbuq_b = sbuf("sbuq", [128, NDT], F32)
        sbrq_t, sbrq_b = sbuf("sbrq", [128, NDT], F32)
        enc_t, enc_b = sbuf("encS", [128, NDT, 40], BF16)
        ws_t, ws_b = sbuf("wsum", [128, NDT, 40], BF16)
        gt_t, gt_b = sbuf("gtS", [40, D], BF16)

        for nm in biases:
            t, bb, src = biases[nm]
            Sc.dma("sp", lambda e, t=t, src=src: e.dma_start(out=t[:, :], in_=src[:, :]), bb)
        Sc.dma("sp", lambda e: e.dma_start(out=ident_t[:, :], in_=identc[:, :]), ident_b)
        Sc.dma("sp", lambda e: e.dma_start(out=ones_t[:, :], in_=onesc[:, :]), ones_b)
        bt, bb_, _ = biases["buq"]
        Sc.op("dve", lambda e: e.tensor_scalar_mul(sbuq_t[:, :], bt[:, :], SCALER),
              reads=[bb_], writes=[sbuq_b])
        bt2, bb2, _ = biases["brq2"]
        Sc.op("dve", lambda e: e.tensor_scalar_mul(sbrq_t[:, :], bt2[:, :], SCALER),
              reads=[bb2], writes=[sbrq_b])
        Sc.dma("gp", lambda e: e.dma_start(
            out=enc_t[:, :, :],
            in_=enct.rearrange("(t p) r -> p t r", p=128)), enc_b)

        with ExitStack() as p0:
            cof_t = p0.enter_context(nc.sbuf_tensor("cof", [128, NDT, 120], F32))
            msk_t = p0.enter_context(nc.sbuf_tensor("msk", [128, NDT, 120], F32))
            cof_bb, msk_bb = Buf("cof"), Buf("msk")
            Sc.dma("sp", lambda e: e.dma_start(
                out=cof_t[:, :, 0:96], in_=co0.rearrange("(t p) r -> p t r", p=128)), cof_bb)
            Sc.dma("sp", lambda e: e.dma_start(
                out=cof_t[:, :, 96:120], in_=co1.rearrange("(t p) r -> p t r", p=128)), cof_bb)
            Sc.dma("gp", lambda e: e.dma_start(
                out=msk_t[:, :, 0:96], in_=mk0.rearrange("(t p) r -> p t r", p=128)), msk_bb)
            Sc.dma("gp", lambda e: e.dma_start(
                out=msk_t[:, :, 96:120], in_=mk1.rearrange("(t p) r -> p t r", p=128)), msk_bb)
            prod_t = p0.enter_context(nc.sbuf_tensor("prod", [128, NDT, 120], F32))
            prod_bb = Buf("prod")
            Sc.op("dve", lambda e: e.tensor_mul(prod_t[:, :, :], cof_t[:, :, :], msk_t[:, :, :]),
                  reads=[cof_bb, msk_bb], writes=[prod_bb])
            tmp_t = p0.enter_context(nc.sbuf_tensor("tmpw", [128, NDT, 40], F32))
            tmp_bb = Buf("tmpw")
            Sc.op("dve", lambda e: e.tensor_add(tmp_t[:, :, 0:32], prod_t[:, :, 0:32],
                                                prod_t[:, :, 32:64]),
                  reads=[prod_bb], writes=[tmp_bb])
            Sc.op("dve", lambda e: e.tensor_add(tmp_t[:, :, 0:32], tmp_t[:, :, 0:32],
                                                prod_t[:, :, 64:96]),
                  reads=[prod_bb, tmp_bb], writes=[tmp_bb])
            Sc.op("dve", lambda e: e.tensor_add(tmp_t[:, :, 32:40], prod_t[:, :, 96:104],
                                                prod_t[:, :, 104:112]),
                  reads=[prod_bb, tmp_bb], writes=[tmp_bb])
            Sc.op("dve", lambda e: e.tensor_add(tmp_t[:, :, 32:40], tmp_t[:, :, 32:40],
                                                prod_t[:, :, 112:120]),
                  reads=[prod_bb, tmp_bb], writes=[tmp_bb])
            Sc.op("dve", lambda e: e.tensor_copy(ws_t[:, :, :], tmp_t[:, :, :]),
                  reads=[tmp_bb], writes=[ws_b])
            mt_ring = [p0.enter_context(nc.sbuf_tensor(f"mtr{i}", [128, 512], BF16))
                       for i in range(2)]
            mt_rb = [Buf(f"mtr{i}") for i in range(2)]
            psg = p0.enter_context(nc.psum_tensor("psg", [40, 512], F32))
            psg_b = Buf("psg")
            for jc in range(8):
                for ot in range(NDT):
                    sl = (jc * NDT + ot) % 2
                    Sc.dma("sp", lambda e, ot=ot, jc=jc, sl=sl: e.dma_start(
                        out=mt_ring[sl][:, :],
                        in_=mt[ot * 128:(ot + 1) * 128, jc * 512:(jc + 1) * 512]),
                        mt_rb[sl])
                    Sc.op("pe", lambda e, ot=ot, sl=sl: e.matmul(
                        psg[:, :], ws_t[:, ot, :], mt_ring[sl][:, :],
                        start=(ot == 0), stop=(ot == NDT - 1)),
                        reads=[ws_b, mt_rb[sl]], writes=[psg_b])
                Sc.op("act", lambda e, jc=jc: e.activation(
                    gt_t[:, jc * 512:(jc + 1) * 512], psg[:, :], AF.Identity,
                    bias=0.0, scale=0.5),
                    reads=[psg_b], writes=[gt_b])
        Sc.barrier()

        with ExitStack() as p1:
            xb_t = [p1.enter_context(nc.sbuf_tensor(f"xb{i}", [128, NDT, 512], BF16))
                    for i in range(2)]
            xb_b = [Buf(f"xb{i}") for i in range(2)]
            xn_t = [p1.enter_context(nc.sbuf_tensor(f"xn{i}", [128, NDT, 512], BF16))
                    for i in range(2)]
            xn_b = [Buf(f"xn{i}") for i in range(2)]
            sq_t = [p1.enter_context(nc.sbuf_tensor(f"sq{i}", [128, 512], BF16))
                    for i in range(4)]
            sq_b = [Buf(f"sq{i}") for i in range(4)]
            rows_t = p1.enter_context(nc.sbuf_tensor("rows", [1, 8, 512], F32))
            rows_b = Buf("rows")
            rowb_t = p1.enter_context(nc.sbuf_tensor("rowb", [1, 2, 512], BF16))
            rowb_b = Buf("rowb")
            mB_t = p1.enter_context(nc.sbuf_tensor("mB", [128, 512], BF16))
            mB_b = Buf("mB")
            rB_t = p1.enter_context(nc.sbuf_tensor("rB", [128, 512], BF16))
            rB_b = Buf("rB")
            ps_s = p1.enter_context(nc.psum_tensor("ps_s", [1, 512], F32))
            ps_q = p1.enter_context(nc.psum_tensor("ps_q", [1, 512], F32))
            ps_b1 = p1.enter_context(nc.psum_tensor("ps_b1", [128, 512], F32))
            ps_b2 = p1.enter_context(nc.psum_tensor("ps_b2", [128, 512], F32))
            ps_sb, ps_qb, ps_b1b, ps_b2b = (Buf(x) for x in
                                            ["ps_s", "ps_q", "ps_b1", "ps_b2"])
            xt_r = xt.rearrange("(t p) c -> p t c", p=128)
            for ch in range(16):
                par = ch % 2
                c0 = ch * 512
                for g in range(4):
                    Sc.dma("gp", lambda e, g=g, c0=c0, par=par: e.dma_start(
                        out=xb_t[par][:, g * 8:(g + 1) * 8, :],
                        in_=xt_r[:, g * 8:(g + 1) * 8, c0:c0 + 512]), xb_b[par],
                        tag="p1a_x")
                for dt in range(NDT):
                    sqs = (ch * NDT + dt) % 4
                    Sc.op("act", lambda e, dt=dt, par=par, sqs=sqs: e.activation(
                        sq_t[sqs][:, :], xb_t[par][:, dt, :], AF.Square),
                        reads=[xb_b[par]], writes=[sq_b[sqs]])
                    Sc.op("pe", lambda e, dt=dt, par=par: e.matmul(
                        ps_s[:, :], ones_t[:, 0:1], xb_t[par][:, dt, :],
                        start=(dt == 0), stop=(dt == NDT - 1)),
                        reads=[ones_b, xb_b[par]], writes=[ps_sb])
                    Sc.op("pe", lambda e, dt=dt, sqs=sqs: e.matmul(
                        ps_q[:, :], ones_t[:, 0:1], sq_t[sqs][:, :],
                        start=(dt == 0), stop=(dt == NDT - 1)),
                        reads=[ones_b, sq_b[sqs]], writes=[ps_qb])
                Sc.op("dve", lambda e: e.tensor_scalar_mul(rows_t[:, 0, :], ps_s[:, :], 1.0 / D),
                      reads=[ps_sb], writes=[rows_b])
                Sc.op("dve", lambda e: e.tensor_scalar_mul(rows_t[:, 1, :], ps_q[:, :], 1.0 / D),
                      reads=[ps_qb], writes=[rows_b])
                Sc.op("dve", lambda e: e.tensor_mul(rows_t[:, 2, :], rows_t[:, 0, :],
                                                    rows_t[:, 0, :]),
                      reads=[rows_b], writes=[rows_b])
                Sc.op("dve", lambda e: e.tensor_sub(rows_t[:, 3, :], rows_t[:, 1, :],
                                                    rows_t[:, 2, :]),
                      reads=[rows_b], writes=[rows_b])
                Sc.op("dve", lambda e: e.tensor_scalar_add(rows_t[:, 3, :],
                                                            rows_t[:, 3, :], EPS),
                      reads=[rows_b], writes=[rows_b])
                Sc.op("act", lambda e: e.activation(rows_t[:, 4, :], rows_t[:, 3, :],
                                                    AF.Ln),
                      reads=[rows_b], writes=[rows_b])
                Sc.op("act", lambda e: e.activation(rows_t[:, 5, :], rows_t[:, 4, :],
                                                    AF.Exp, scale=-0.5),
                      reads=[rows_b], writes=[rows_b])
                Sc.op("dve", lambda e: e.tensor_copy(rowb_t[:, 0, :], rows_t[:, 0, :]),
                      reads=[rows_b], writes=[rowb_b])
                Sc.op("dve", lambda e: e.tensor_copy(rowb_t[:, 1, :], rows_t[:, 5, :]),
                      reads=[rows_b], writes=[rowb_b])
                Sc.op("pe", lambda e: e.matmul(ps_b1[:, :], ones_t[0:1, :], rowb_t[:, 0, :],
                                               start=True, stop=True),
                      reads=[ones_b, rowb_b], writes=[ps_b1b])
                Sc.op("pe", lambda e: e.matmul(ps_b2[:, :], ones_t[0:1, :], rowb_t[:, 1, :],
                                               start=True, stop=True),
                      reads=[ones_b, rowb_b], writes=[ps_b2b])
                Sc.op("dve", lambda e: e.tensor_copy(mB_t[:, :], ps_b1[:, :]),
                      reads=[ps_b1b], writes=[mB_b])
                Sc.op("dve", lambda e: e.tensor_copy(rB_t[:, :], ps_b2[:, :]),
                      reads=[ps_b2b], writes=[rB_b])
                for dt in range(NDT):
                    Sc.op("dve", lambda e, dt=dt, par=par: e.tensor_sub(
                        xn_t[par][:, dt, :], xb_t[par][:, dt, :], mB_t[:, :]),
                        reads=[xb_b[par], mB_b], writes=[xn_b[par]])
                    Sc.op("dve", lambda e, dt=dt, par=par: e.tensor_mul(
                        xn_t[par][:, dt, :], xn_t[par][:, dt, :], rB_t[:, :]),
                        reads=[xn_b[par], rB_b], writes=[xn_b[par]])
                for g in range(4):
                    Sc.dma("sp", lambda e, g=g, c0=c0, par=par: e.dma_start(
                        out=xnb.rearrange("(t p) c -> p t c", p=128)[
                            :, g * 8:(g + 1) * 8, c0:c0 + 512],
                        in_=xn_t[par][:, g * 8:(g + 1) * 8, :]),
                        dB["xnb"], reads=[xn_b[par]], tag="p1a_st")
        Sc.barrier()

        s_cq = ExitStack()
        cq_t = s_cq.enter_context(nc.sbuf_tensor("cqS", [128, 12, QOWN], BF16))
        cq_b = Buf("cqS")
        with ExitStack() as p1c:
            wq_t = p1c.enter_context(nc.sbuf_tensor("wqS", [128, NDT, DC1], BF16))
            wq_b = Buf("wqS")
            for dt in range(NDT):
                Sc.dma("gp", lambda e, dt=dt: e.dma_start(
                    out=wq_t[:, dt, :], in_=wdq[dt * 128:(dt + 1) * 128, :]), wq_b)
            xc_t = [p1c.enter_context(nc.sbuf_tensor("xc0", [128, NDT, 512], BF16))]
            xc_b = [Buf("xc0")]
            xnb_r = xnb.rearrange("(t p) c -> p t c", p=128)
            psc = [p1c.enter_context(nc.psum_tensor(f"psc{i}", [128, 512], F32))
                   for i in range(2)]
            psc_b = [Buf(f"psc{i}") for i in range(2)]
            bt_dq = biases["bdq"]
            for ch in range(4):
                par = 0
                c0 = ch * 512
                for g in range(4):
                    Sc.dma("sp", lambda e, g=g, c0=c0, par=par: e.dma_start(
                        out=xc_t[par][:, g * 8:(g + 1) * 8, :],
                        in_=xnb_r[:, g * 8:(g + 1) * 8, c0:c0 + 512]), xc_b[par])
                for ct in range(12):
                    pp = ct % 2
                    for dt in range(NDT):
                        Sc.op("pe", lambda e, ct=ct, dt=dt, par=par, pp=pp: e.matmul(
                            psc[pp][:, :], wq_t[:, dt, ct * 128:(ct + 1) * 128],
                            xc_t[par][:, dt, :],
                            start=(dt == 0), stop=(dt == NDT - 1)),
                            reads=[wq_b, xc_b[par]], writes=[psc_b[pp]])
                    Sc.op("act", lambda e, ct=ct, c0=c0, pp=pp: e.activation(
                        cq_t[:, ct, c0:c0 + 512], psc[pp][:, :], AF.Identity,
                        bias=bt_dq[0][:, ct:ct + 1]),
                        reads=[psc_b[pp], bt_dq[1]], writes=[cq_b])
        Sc.barrier()

        with ExitStack() as p2:
            wh_t = [p2.enter_context(nc.sbuf_tensor(f"wh{i}", [128, 12, 256], BF16))
                    for i in range(2)]
            wh_b = [Buf(f"wh{i}") for i in range(2)]
            psq = [p2.enter_context(nc.psum_tensor(f"psq{i}", [128, 512], F32))
                   for i in range(4)]
            psq_b = [Buf(f"psq{i}") for i in range(4)]
            qst = [p2.enter_context(nc.sbuf_tensor(f"qst{i}", [128, QOWN], BF16))
                   for i in range(2)]
            qst_b = [Buf(f"qst{i}") for i in range(2)]
            qsr = [p2.enter_context(nc.sbuf_tensor(f"qsr{i}", [128, QOWN], BF16))
                   for i in range(2)]
            qsr_b = [Buf(f"qsr{i}") for i in range(2)]
            for h in range(H):
                par = h % 2
                Sc.dma("gp", lambda e, h=h, par=par: e.dma_start(
                    out=wh_t[par][:, :, 0:128],
                    in_=wuq.rearrange("(t p) f -> p t f", p=128)[
                        :, :, h * 128:(h + 1) * 128]), wh_b[par])
                Sc.dma("gp", lambda e, h=h, par=par: e.dma_start(
                    out=wh_t[par][:, :, 128:256],
                    in_=wrq2.rearrange("(t p) f -> p t f", p=128)[
                        :, :, h * 128:(h + 1) * 128]), wh_b[par])
                back_q = []

                def q_group(g, par=par, h=h):
                    qn, which = g // 2, g % 2
                    q0 = qn * 512
                    pp = g % 4
                    c0, c1 = (0, 128) if which == 0 else (128, 256)
                    for ct in range(12):
                        Sc.op("pe", lambda e, ct=ct, q0=q0, par=par, pp=pp,
                              c0=c0, c1=c1: e.matmul(
                            psq[pp][:, :], wh_t[par][:, ct, c0:c1],
                            cq_t[:, ct, q0:q0 + 512],
                            start=(ct == 0), stop=(ct == 11)),
                            reads=[wh_b[par], cq_b], writes=[psq_b[pp]])

                    def ev(e, h=h, q0=q0, par=par, pp=pp, which=which):
                        dst = qst if which == 0 else qsr
                        bia = sbuq_t if which == 0 else sbrq_t
                        return e.activation(
                            dst[par][:, q0:q0 + 512], psq[pp][:, :], AF.Identity,
                            bias=bia[:, h:h + 1], scale=SCALER)
                    back_q.append((ev, pp, which))

                def q_back(par=par):
                    ev, pp, which = back_q.pop(0)
                    bb = sbuq_b if which == 0 else sbrq_b
                    db = qst_b if which == 0 else qsr_b
                    Sc.op("act", ev, reads=[psq_b[pp], bb], writes=[db[par]])

                for g in range(8):
                    q_group(g)
                    if g >= 2:
                        q_back()
                q_back()
                q_back()
                Sc.dma("sp", lambda e, h=h, par=par: e.dma_start(
                    out=qcm[h * 128:(h + 1) * 128, :], in_=qst[par][:, :]),
                    dB["qcm"], reads=[qst_b[par]])
                Sc.dma("sp", lambda e, h=h, par=par: e.dma_start(
                    out=qcr[h * 128:(h + 1) * 128, :], in_=qsr[par][:, :]),
                    dB["qcr"], reads=[qsr_b[par]])
        Sc.barrier()
        s_cq.close()

        s_kv = ExitStack()
        ckv_t = s_kv.enter_context(nc.sbuf_tensor("ckvS", [128, 4, S], BF16))
        ckv_b = Buf("ckvS")
        krd_t = s_kv.enter_context(nc.sbuf_tensor("krdS", [128, S], BF16))
        krd_b = Buf("krdS")
        with ExitStack() as p1b:
            wkv_t = p1b.enter_context(nc.sbuf_tensor("wkvS", [128, NDT, DC], BF16))
            wkv_b = Buf("wkvS")
            wrk_t = p1b.enter_context(nc.sbuf_tensor("wrkS", [128, NDT, 128], BF16))
            wrk_b = Buf("wrkS")
            for dt in range(NDT):
                Sc.dma("gp", lambda e, dt=dt: e.dma_start(
                    out=wkv_t[:, dt, :], in_=wdkv[dt * 128:(dt + 1) * 128, :]), wkv_b)
                Sc.dma("gp", lambda e, dt=dt: e.dma_start(
                    out=wrk_t[:, dt, :], in_=wrk2[dt * 128:(dt + 1) * 128, :]), wrk_b)
            xk_t = [p1b.enter_context(nc.sbuf_tensor(f"xk{i}", [128, NDT, 512], BF16))
                    for i in range(2)]
            xk_b = [Buf(f"xk{i}") for i in range(2)]
            psk = [p1b.enter_context(nc.psum_tensor(f"psk{i}", [128, 512], F32))
                   for i in range(5)]
            psk_b = [Buf(f"psk{i}") for i in range(5)]
            bt_kv = biases["bdkv"]
            bt_rk = biases["brk2"]
            xnb_r2 = xnb.rearrange("(t p) c -> p t c", p=128)
            for ch in range(16):
                par = ch % 2
                c0 = ch * 512
                for g in range(4):
                    Sc.dma("sp", lambda e, g=g, c0=c0, par=par: e.dma_start(
                        out=xk_t[par][:, g * 8:(g + 1) * 8, :],
                        in_=xnb_r2[:, g * 8:(g + 1) * 8, c0:c0 + 512]), xk_b[par])
                for dt in range(NDT):
                    for ct in range(4):
                        Sc.op("pe", lambda e, ct=ct, dt=dt, par=par: e.matmul(
                            psk[ct][:, :], wkv_t[:, dt, ct * 128:(ct + 1) * 128],
                            xk_t[par][:, dt, :],
                            start=(dt == 0), stop=(dt == NDT - 1)),
                            reads=[wkv_b, xk_b[par]], writes=[psk_b[ct]])
                    Sc.op("pe", lambda e, dt=dt, par=par: e.matmul(
                        psk[4][:, :], wrk_t[:, dt, :], xk_t[par][:, dt, :],
                        start=(dt == 0), stop=(dt == NDT - 1)),
                        reads=[wrk_b, xk_b[par]], writes=[psk_b[4]])
                for ct in range(4):
                    Sc.op("act", lambda e, ct=ct, c0=c0: e.activation(
                        ckv_t[:, ct, c0:c0 + 512], psk[ct][:, :], AF.Identity,
                        bias=bt_kv[0][:, ct:ct + 1]),
                        reads=[psk_b[ct], bt_kv[1]], writes=[ckv_b])
                Sc.op("act", lambda e, c0=c0: e.activation(
                    krd_t[:, c0:c0 + 512], psk[4][:, :], AF.Identity,
                    bias=bt_rk[0][:, 0:1]),
                    reads=[psk_b[4], bt_rk[1]], writes=[krd_b])
        Sc.barrier()

        with ExitStack() as p1v:
            wv_t = [p1v.enter_context(nc.sbuf_tensor(f"wv{i}", [128, 4, 128], BF16))
                    for i in range(2)]
            wv_b = [Buf(f"wv{i}") for i in range(2)]
            vst_t = [p1v.enter_context(nc.sbuf_tensor(f"vst{i}", [128, 512], BF16))
                     for i in range(2)]
            vst_b = [Buf(f"vst{i}") for i in range(2)]
            vsg_t = [p1v.enter_context(nc.sbuf_tensor(f"vsg{i}", [128, 4, 128], BF16))
                     for i in range(2)]
            vsg_b = [Buf(f"vsg{i}") for i in range(2)]
            psA = [p1v.enter_context(nc.psum_tensor(f"psA{i}", [128, 512], F32))
                   for i in range(2)]
            psA_b = [Buf(f"psA{i}") for i in range(2)]
            psV = [p1v.enter_context(nc.psum_tensor(f"psV{i}", [128, 512], BF16))
                   for i in range(2)]
            psV_b = [Buf(f"psV{i}") for i in range(2)]
            bt_uv = biases["buv"]
            vtk_r = vtk.rearrange("h (c p) d -> h p c d", p=128)
            def v_mm(h, kc):
                par, sl, k0 = h % 2, kc % 2, kc * 512
                for ct in range(4):
                    Sc.op("pe", lambda e, ct=ct, k0=k0, par=par, sl=sl: e.matmul(
                        psA[sl][:, :], wv_t[par][:, ct, :],
                        ckv_t[:, ct, k0:k0 + 512],
                        start=(ct == 0), stop=(ct == 3)),
                        reads=[wv_b[par], ckv_b], writes=[psA_b[sl]])

            def v_back(h, kc):
                sl = kc % 2
                Sc.op("act", lambda e, h=h, sl=sl: e.activation(
                    vst_t[sl][:, :], psA[sl][:, :], AF.Identity,
                    bias=bt_uv[0][:, h:h + 1]),
                    reads=[psA_b[sl], bt_uv[1]], writes=[vst_b[sl]])
                for i in range(4):
                    Sc.op("pe", lambda e, i=i, sl=sl: e.transpose(
                        psV[sl][:, i * 128:(i + 1) * 128],
                        vst_t[sl][:, i * 128:(i + 1) * 128], ident_t[:, :]),
                        reads=[vst_b[sl], ident_b], writes=[psV_b[sl]])
                Sc.op("dve", lambda e, sl=sl: e.tensor_copy(
                    vsg_t[sl][:, :, :], psV[sl][:, :]),
                    reads=[psV_b[sl]], writes=[vsg_b[sl]])
                Sc.dma("sp", lambda e, h=h, kc=kc, sl=sl: e.dma_start(
                    out=vtk_r[h, :, kc * 4:(kc + 1) * 4, :], in_=vsg_t[sl][:, :, :]),
                    dB["vtk"], reads=[vsg_b[sl]])

            steps = [(h, kc) for h in range(H) for kc in range(16)]
            for h in range(2):
                Sc.dma("gp", lambda e, h=h: e.dma_start(
                    out=wv_t[h][:, :, :],
                    in_=wuv.rearrange("(t p) f -> p t f", p=128)[
                        :, :, h * 128:(h + 1) * 128]), wv_b[h])
            for i, (h, kc) in enumerate(steps):
                if i >= 1:
                    v_back(*steps[i - 1])
                v_mm(h, kc)
                if kc == 15 and h + 2 < H:
                    hh = h + 2
                    Sc.dma("gp", lambda e, hh=hh: e.dma_start(
                        out=wv_t[hh % 2][:, :, :],
                        in_=wuv.rearrange("(t p) f -> p t f", p=128)[
                            :, :, hh * 128:(hh + 1) * 128]), wv_b[hh % 2])
            v_back(*steps[-1])
        Sc.barrier()

        with ExitStack() as p3:
            kbuf = [p3.enter_context(nc.sbuf_tensor(f"kb{i}", [128, S], BF16))
                    for i in range(2)]
            kbuf_b = [Buf(f"kb{i}") for i in range(2)]
            vbuf = [p3.enter_context(nc.sbuf_tensor(f"vb{i}", [128, NKT, 128], BF16))
                    for i in range(2)]
            vbuf_b = [Buf(f"vb{i}") for i in range(2)]
            qm_t = [p3.enter_context(nc.sbuf_tensor(f"qm{i}", [128, QOWN], BF16))
                    for i in range(2)]
            qm_b = [Buf(f"qm{i}") for i in range(2)]
            qr_t = [p3.enter_context(nc.sbuf_tensor(f"qr{i}", [128, QOWN], BF16))
                    for i in range(2)]
            qr_b = [Buf(f"qr{i}") for i in range(2)]
            wk_t = [p3.enter_context(nc.sbuf_tensor(f"wk{i}", [128, 4, 128], BF16))
                    for i in range(2)]
            wk_b = [Buf(f"wk{i}") for i in range(2)]
            ptp_t = [p3.enter_context(nc.sbuf_tensor(f"ptp{i}", [128, 1024], BF16))
                     for i in range(3)]
            ptp_hb = [[Buf(f"ptph{i}{j}") for j in range(2)] for i in range(3)]
            zacc_t = [p3.enter_context(nc.sbuf_tensor(f"zac{i}", [128, 1024], F32))
                      for i in range(2)]
            zacc_b = [Buf(f"zac{i}") for i in range(2)]
            zacg_t = [p3.enter_context(nc.sbuf_tensor(f"zag{i}", [128, 1024], F32))
                      for i in range(2)]
            zacg_b = [Buf(f"zag{i}") for i in range(2)]
            ost_t = [p3.enter_context(nc.sbuf_tensor(f"ost{i}", [128, 1024], BF16))
                     for i in range(2)]
            ost_b = [Buf(f"ost{i}") for i in range(2)]
            zbf_t = p3.enter_context(nc.sbuf_tensor("zbf", [1, 2, 1024], BF16))
            zbf_b = [Buf(f"zbf{i}") for i in range(2)]
            psS = [p3.enter_context(nc.psum_tensor(f"psS{i}", [128, 1024], F32))
                   for i in range(2)]
            psS_b = [Buf(f"psS{i}") for i in range(2)]
            psS_hb = [[Buf(f"psSh{i}{j}") for j in range(2)] for i in range(2)]
            psZ = p3.enter_context(nc.psum_tensor("psZ", [1, 1024], F32))
            psZ_b = Buf("psZ")
            psO = p3.enter_context(nc.psum_tensor("psO", [128, 1024], F32))
            psO_b = Buf("psO")
            bt_uk = biases["buk"]
            vtk_r2 = vtk.rearrange("h (c p) d -> h p c d", p=128)

            pending_tail = [None]

            def flush_tail():
                if pending_tail[0] is not None:
                    pending_tail[0]()
                    pending_tail[0] = None

            for h in range(H):
                par = h % 2
                Sc.dma("sp", lambda e, h=h, par=par: e.dma_start(
                    out=qm_t[par][:, :], in_=qcm[h * 128:(h + 1) * 128, :]), qm_b[par])
                Sc.dma("sp", lambda e, h=h, par=par: e.dma_start(
                    out=qr_t[par][:, :], in_=qcr[h * 128:(h + 1) * 128, :]), qr_b[par])
                Sc.dma("gp", lambda e, h=h, par=par: e.dma_start(
                    out=wk_t[par][:, :, :],
                    in_=wuk.rearrange("(t p) f -> p t f", p=128)[
                        :, :, h * 128:(h + 1) * 128]), wk_b[par])
                for g in range(4):
                    Sc.dma("gp", lambda e, h=h, par=par, g=g: e.dma_start(
                        out=vbuf[par][:, g * 16:(g + 1) * 16, :],
                        in_=vtk_r2[h, :, g * 16:(g + 1) * 16, :]), vbuf_b[par],
                        reads=[dB["vtk"]])
                for kc in range(16):
                    k0 = kc * 512
                    t_, hf = kc % 2, (kc // 2) % 2
                    for ct in range(4):
                        Sc.op("pe", lambda e, ct=ct, k0=k0, par=par, t_=t_,
                              hf=hf: e.matmul(
                            psS[t_][:, hf * 512:(hf + 1) * 512], wk_t[par][:, ct, :],
                            ckv_t[:, ct, k0:k0 + 512],
                            start=(ct == 0), stop=(ct == 3)),
                            reads=[wk_b[par], ckv_b], writes=[psS_hb[t_][hf]])
                    Sc.op("act", lambda e, h=h, k0=k0, par=par, t_=t_,
                          hf=hf: e.activation(
                        kbuf[par][:, k0:k0 + 512], psS[t_][:, hf * 512:(hf + 1) * 512],
                        AF.Identity, bias=bt_uk[0][:, h:h + 1]),
                        reads=[psS_hb[t_][hf], bt_uk[1]], writes=[kbuf_b[par]])

                for qp in range(2):
                    q0 = qp * 1024
                    so = qp % 2

                    def emit_S(kt, par=par, q0=q0):
                        sl = kt % 2
                        Sc.op("pe", lambda e, kt=kt, q0=q0, par=par, sl=sl: e.matmul(
                            psS[sl][:, 0:512],
                            kbuf[par][:, kt * 128:(kt + 1) * 128],
                            qm_t[par][:, q0:q0 + 512], start=True, stop=False),
                            reads=[kbuf_b[par], qm_b[par]], writes=[psS_hb[sl][0]])
                        Sc.op("pe", lambda e, kt=kt, q0=q0, par=par, sl=sl: e.matmul(
                            psS[sl][:, 512:1024],
                            kbuf[par][:, kt * 128:(kt + 1) * 128],
                            qm_t[par][:, q0 + 512:q0 + 1024], start=True, stop=False),
                            reads=[kbuf_b[par], qm_b[par]], writes=[psS_hb[sl][1]])
                        Sc.op("pe", lambda e, kt=kt, q0=q0, par=par, sl=sl: e.matmul(
                            psS[sl][:, 0:512],
                            krd_t[0:64, kt * 128:(kt + 1) * 128],
                            qr_t[par][0:64, q0:q0 + 512], start=False, stop=True,
                            tile_position=(0, 0)),
                            reads=[krd_b, qr_b[par]], writes=[psS_hb[sl][0]])
                        Sc.op("pe", lambda e, kt=kt, q0=q0, par=par, sl=sl: e.matmul(
                            psS[sl][:, 512:1024],
                            krd_t[64:128, kt * 128:(kt + 1) * 128],
                            qr_t[par][64:128, q0 + 512:q0 + 1024], start=False,
                            stop=True, tile_position=(64, 0)),
                            reads=[krd_b, qr_b[par]], writes=[psS_hb[sl][1]])

                    def emit_back(kt, par=par, so=so):
                        s2, s3 = kt % 2, kt % 3
                        Sc.op("act", lambda e, s2=s2, s3=s3: e.activation(
                            ptp_t[s3][:, :], psS[s2][:, :], AF.Exp),
                            reads=[psS_hb[s2][0], psS_hb[s2][1]],
                            writes=[ptp_hb[s3][0], ptp_hb[s3][1]],
                            tag=f"exp_h{h}_q{q0}_kt{kt}")
                        for half in range(2):
                            Sc.op("pe", lambda e, kt=kt, s3=s3, half=half: e.matmul(
                                psO[:, half * 512:(half + 1) * 512],
                                vbuf[par][:, kt, :],
                                ptp_t[s3][:, half * 512:(half + 1) * 512],
                                start=(kt == 0), stop=(kt == NKT - 1)),
                                reads=[vbuf_b[par], ptp_hb[s3][half]], writes=[psO_b])
                        zt = zacc_t if kt % 2 == 0 else zacg_t
                        zb = zacc_b if kt % 2 == 0 else zacg_b
                        if kt < 2:
                            Sc.op("dve", lambda e, s3=s3, so=so, zt=zt:
                                  e.tensor_copy(zt[so][:, :], ptp_t[s3][:, :]),
                                  reads=[ptp_hb[s3][0], ptp_hb[s3][1]],
                                  writes=[zb[so]])
                        else:
                            zeng = "dve" if kt % 2 == 0 else "gp"
                            Sc.op(zeng, lambda e, s3=s3, so=so, zt=zt:
                                  e.tensor_add(zt[so][:, :], zt[so][:, :],
                                               ptp_t[s3][:, :]),
                                  reads=[ptp_hb[s3][0], ptp_hb[s3][1], zb[so]],
                                  writes=[zb[so]])
                        if h == 0 and q0 == 0 and kt == 0 and "dbgp" in debug_outputs:
                            Sc.dma("sp", lambda e, s3=s3: e.dma_start(
                                out=dbgp[:, :], in_=ptp_t[s3][:, :]),
                                dB["dbgp"], reads=[ptp_hb[s3][0], ptp_hb[s3][1]])
                            Sc.dma("sp", lambda e, par=par: e.dma_start(
                                out=dbgk[:, :], in_=kbuf[par][:, 0:512]),
                                dB["dbgk"], reads=[kbuf_b[par]])
                            Sc.dma("sp", lambda e: e.dma_start(
                                out=dbgr[:, :], in_=krd_t[:, 0:512]),
                                dB["dbgr"], reads=[krd_b])

                    LAG = 2
                    for kt in range(NKT):
                        if kt >= LAG:
                            emit_back(kt - LAG)
                        emit_S(kt)
                        if kt == 1:
                            flush_tail()
                    for kt in range(NKT - LAG, NKT):
                        emit_back(kt)
                    Sc.op("dve", lambda e, so=so: e.tensor_copy(
                        ost_t[so][:, :], psO[:, :]),
                        reads=[psO_b], writes=[ost_b[so]])
                    Sc.op("dve", lambda e, so=so: e.tensor_add(
                        ost_t[1 - so][:, :], zacc_t[so][:, :], zacg_t[so][:, :]),
                        reads=[zacc_b[so], zacg_b[so]], writes=[ost_b[1 - so]])
                    for half in range(2):
                        Sc.op("pe", lambda e, half=half, so=so: e.matmul(
                            psZ[:, half * 512:(half + 1) * 512], ones_t[:, 0:1],
                            ost_t[1 - so][:, half * 512:(half + 1) * 512],
                            start=True, stop=True),
                            reads=[ones_b, ost_b[1 - so]], writes=[psZ_b])
                    def _zc(e, so=so):
                        with nc.allow_low_precision(reason="Z is stored bf16"):
                            return e.tensor_copy(zbf_t[:, so, :], psZ[:, :])
                    Sc.op("dve", _zc, reads=[psZ_b], writes=[zbf_b[so]])

                    def tail_b(h=h, q0=q0, so=so):
                        Sc.dma("sp", lambda e, h=h, q0=q0, so=so: e.dma_start(
                            out=zrows[h, q0:q0 + 1024], in_=zbf_t[0:1, so, :]),
                            dB["zrows"], reads=[zbf_b[so]])
                        Sc.dma("sp", lambda e, h=h, q0=q0, so=so: e.dma_start(
                            out=oh[h * 128:(h + 1) * 128, q0:q0 + 1024],
                            in_=ost_t[so][:, :]),
                            dB["oh"], reads=[ost_b[so]])
                    pending_tail[0] = tail_b
            flush_tail()

        Sc.barrier()
        s_kv.close()

        with ExitStack() as p4:
            ohr_t = p4.enter_context(nc.sbuf_tensor("ohr", [128, NDT, QOWN], BF16))
            ohr_b = Buf("ohr")
            zr_t = p4.enter_context(nc.sbuf_tensor("zrS", [1, 2, QOWN], BF16))
            zr_b = [Buf(f"zrS{i}") for i in range(2)]
            zbc_t = [p4.enter_context(nc.sbuf_tensor(f"zbc{i}", [128, 512], F32))
                     for i in range(2)]
            zbc_b = [Buf(f"zbc{i}") for i in range(2)]
            psB4 = p4.enter_context(nc.psum_tensor("psB4", [128, 512], F32))
            psB4_b = Buf("psB4")
            oh_r = oh.rearrange("(t p) c -> p t c", p=128)
            for g in range(8):
                Sc.dma("sp", lambda e, g=g: e.dma_start(
                    out=ohr_t[:, g * 4:(g + 1) * 4, :],
                    in_=oh_r[:, g * 4:(g + 1) * 4, :]), ohr_b, reads=[dB["oh"]])
            for dt in range(NDT):
                zs = dt % 2
                Sc.dma("sp", lambda e, dt=dt, zs=zs: e.dma_start(
                    out=zr_t[:, zs, :], in_=zrows[dt, :]), zr_b[zs],
                    reads=[dB["zrows"]])
                for qn in range(4):
                    zp = qn % 2
                    Sc.op("pe", lambda e, dt=dt, qn=qn, zs=zs: e.matmul(
                        psB4[:, :], ones_t[0:1, :],
                        zr_t[0:1, zs, qn * 512:(qn + 1) * 512], start=True, stop=True),
                        reads=[ones_b, zr_b[zs]], writes=[psB4_b])
                    Sc.op("dve", lambda e, zp=zp: e.reciprocal(
                        zbc_t[zp][:, :], psB4[:, :]),
                        reads=[psB4_b], writes=[zbc_b[zp]])
                    Sc.op("dve", lambda e, dt=dt, qn=qn, zp=zp: e.tensor_mul(
                        ohr_t[:, dt, qn * 512:(qn + 1) * 512],
                        ohr_t[:, dt, qn * 512:(qn + 1) * 512], zbc_t[zp][:, :]),
                        reads=[ohr_b, zbc_b[zp]], writes=[ohr_b])
            wo_t = [p4.enter_context(nc.sbuf_tensor(f"wo{i}", [128, NDT, 128], BF16))
                    for i in range(2)]
            wo_b = [Buf(f"wo{i}") for i in range(2)]
            psP = [p4.enter_context(nc.psum_tensor(f"psP{i}", [128, 512], F32))
                   for i in range(7)]
            psP_b = [Buf(f"psP{i}") for i in range(7)]
            oa_t = [p4.enter_context(nc.sbuf_tensor(f"oaS{i}", [128, 512], F32))
                    for i in range(4)]
            oa_b = [Buf(f"oaS{i}") for i in range(4)]
            bt_o = biases["bout"]
            for ft in range(NDT):
                par = ft % 2
                for g in range(2):
                    Sc.dma("gp", lambda e, ft=ft, g=g, par=par: e.dma_start(
                        out=wo_t[par][:, g * 16:(g + 1) * 16, :],
                        in_=wout.rearrange("(t p) f -> p t f", p=128)[
                            :, g * 16:(g + 1) * 16, ft * 128:(ft + 1) * 128]),
                        wo_b[par])
                for qn in range(4):
                    pp = (ft * 4 + qn) % 7
                    q0 = qn * 512
                    for dt in range(NDT):
                        Sc.op("pe", lambda e, dt=dt, q0=q0, par=par, pp=pp: e.matmul(
                            psP[pp][:, :], wo_t[par][:, dt, :],
                            ohr_t[:, dt, q0:q0 + 512],
                            start=(dt == 0), stop=(dt == NDT - 1)),
                            reads=[wo_b[par], ohr_b], writes=[psP_b[pp]])
                    so = qn % 4
                    Sc.op("act", lambda e, ft=ft, pp=pp, so=so: e.activation(
                        oa_t[so][:, :], psP[pp][:, :], AF.Identity,
                        bias=bt_o[0][:, ft:ft + 1]),
                        reads=[psP_b[pp], bt_o[1]], writes=[oa_b[so]])
                    Sc.dma("sp", lambda e, ft=ft, q0=q0, so=so: e.dma_start(
                        out=oat[ft * 128:(ft + 1) * 128, q0:q0 + 512], in_=oa_t[so][:, :]),
                        dB["oat"], reads=[oa_b[so]])
        Sc.barrier()

        x1s = dt_("x1s", [D, QOWN], F32, kind="Internal")
        dB["x1s"] = Buf("x1s", sink=True)
        ys = dt_("ys", [D, QOWN], BF16, kind="Internal")
        dB["ys"] = Buf("ys", sink=True)
        xns = dt_("xns", [D, QOWN], BF16, kind="Internal")
        dB["xns"] = Buf("xns", sink=True)
        wbc = dt_("wbc", [D, D], BF16, kind="Internal")
        dB["wbc"] = Buf("wbc", sink=True)
        with ExitStack() as p5:
            eh_t = p5.enter_context(nc.sbuf_tensor("ehS", [40, QOWN], BF16))
            eh_b = Buf("ehS")
            r5_t = p5.enter_context(nc.sbuf_tensor("r5", [1, 8, 512], F32))
            r5_b = Buf("r5")
            r5b_t = p5.enter_context(nc.sbuf_tensor("r5b", [1, 2, 512], BF16))
            r5b_b = Buf("r5b")
            mB5_t = p5.enter_context(nc.sbuf_tensor("mB5", [128, 4, 512], F32))
            mB5_b = Buf("mB5")
            rB5_t = p5.enter_context(nc.sbuf_tensor("rB5", [128, 4, 512], F32))
            rB5_b = Buf("rB5")
            st4_t = p5.enter_context(nc.sbuf_tensor("st4", [1, 2, 4, 512], F32))
            st4_b = Buf("st4")
            psB5 = p5.enter_context(nc.psum_tensor("psB5", [128, 512], F32))
            psB5_b = Buf("psB5")
            bt_b = biases["bbase"]
            oat_r = oat.rearrange("(t p) c -> p t c", p=128)
            xt_r5 = xt.rearrange("(t p) c -> p t c", p=128)
            x1s_r = x1s.rearrange("(t p) c -> p t c", p=128)
            xns_r = xns.rearrange("(t p) c -> p t c", p=128)
            ys_r = ys.rearrange("(t p) c -> p t c", p=128)
            out_r = out.rearrange("(t p) c -> p t c", p=128)
            wbt_r = wbt.rearrange("(t p) f -> p t f", p=128)
            wbc_r = wbc.rearrange("(t p) f -> p t f", p=128)

            def bcast_rows(msl, rsl):
                Sc.op("dve", lambda e: e.tensor_copy(r5b_t[:, 0, :], r5_t[:, msl, :]),
                      reads=[r5_b], writes=[r5b_b])
                Sc.op("dve", lambda e: e.tensor_copy(r5b_t[:, 1, :], r5_t[:, rsl, :]),
                      reads=[r5_b], writes=[r5b_b])
                for j in range(4):
                    for i, (dst, dstb) in enumerate(((mB5_t, mB5_b), (rB5_t, rB5_b))):
                        Sc.op("pe", lambda e, i=i: e.matmul(
                            psB5[:, :], ones_t[0:1, :], r5b_t[:, i, :],
                            start=True, stop=True),
                            reads=[ones_b, r5b_b], writes=[psB5_b])
                        Sc.op("dve", lambda e, dst=dst, j=j: e.tensor_copy(
                            dst[:, j, :], psB5[:, :]),
                            reads=[psB5_b], writes=[dstb])

            def rows_from(sum_reads, mk_sum0, mk_sum1):
                Sc.op("dve", mk_sum0, reads=sum_reads, writes=[r5_b])
                Sc.op("dve", mk_sum1, reads=sum_reads, writes=[r5_b])
                Sc.op("dve", lambda e: e.tensor_scalar_mul(r5_t[:, 0, :],
                                                           r5_t[:, 6, :], 1.0 / D),
                      reads=[r5_b], writes=[r5_b])
                Sc.op("dve", lambda e: e.tensor_scalar_mul(r5_t[:, 1, :],
                                                           r5_t[:, 7, :], 1.0 / D),
                      reads=[r5_b], writes=[r5_b])
                Sc.op("dve", lambda e: e.tensor_mul(r5_t[:, 2, :], r5_t[:, 0, :],
                                                    r5_t[:, 0, :]),
                      reads=[r5_b], writes=[r5_b])
                Sc.op("dve", lambda e: e.tensor_sub(r5_t[:, 3, :], r5_t[:, 1, :],
                                                    r5_t[:, 2, :]),
                      reads=[r5_b], writes=[r5_b])
                Sc.op("dve", lambda e: e.tensor_scalar_add(r5_t[:, 3, :],
                                                           r5_t[:, 3, :], EPS),
                      reads=[r5_b], writes=[r5_b])
                Sc.op("act", lambda e: e.activation(r5_t[:, 4, :], r5_t[:, 3, :], AF.Ln),
                      reads=[r5_b], writes=[r5_b])
                Sc.op("act", lambda e: e.activation(r5_t[:, 5, :], r5_t[:, 4, :],
                                                    AF.Exp, scale=-0.5),
                      reads=[r5_b], writes=[r5_b])
                bcast_rows(0, 5)

            with ExitStack() as p5a:
                psL = [p5a.enter_context(nc.psum_tensor(f"psL{i}", [1, 512], F32))
                       for i in range(2)]
                psL_b = [Buf(f"psL{i}") for i in range(2)]
                psE = p5a.enter_context(nc.psum_tensor("psE", [40, 512], F32))
                psE_b = Buf("psE")
                oaCh_t = p5a.enter_context(nc.sbuf_tensor("oaCh", [128, NDT, 512], BF16))
                oaCh_b = Buf("oaCh")
                sqw_t = [p5a.enter_context(nc.sbuf_tensor(f"sqw{i}", [128, 4, 512], BF16))
                         for i in range(2)]
                sqw_b = [Buf(f"sqw{i}") for i in range(2)]
                bfw_t = [p5a.enter_context(nc.sbuf_tensor(f"bfw{i}", [128, 4, 512], BF16))
                         for i in range(2)]
                bfw_b = [Buf(f"bfw{i}") for i in range(2)]
                xrw_t = [p5a.enter_context(nc.sbuf_tensor(f"xrw{i}", [128, 4, 512], F32))
                         for i in range(2)]
                xrw_b = [Buf(f"xrw{i}") for i in range(2)]
                x1w_t = [p5a.enter_context(nc.sbuf_tensor(f"x1w{i}", [128, 4, 512], F32))
                         for i in range(2)]
                x1w_b = [Buf(f"x1w{i}") for i in range(2)]
                for ch in range(4):
                    c0 = ch * 512
                    for g in range(4):
                        Sc.dma("gp", lambda e, g=g, c0=c0: e.dma_start(
                            out=oaCh_t[:, g * 8:(g + 1) * 8, :],
                            in_=oat_r[:, g * 8:(g + 1) * 8, c0:c0 + 512]), oaCh_b,
                            reads=[dB["oat"]])
                    for g in range(8):
                        gp_ = g % 2
                        Sc.op("act", lambda e, g=g, gp_=gp_: e.activation(
                            sqw_t[gp_][:, :, :], oaCh_t[:, g * 4:(g + 1) * 4, :],
                            AF.Square),
                            reads=[oaCh_b], writes=[sqw_b[gp_]])
                        for j in range(4):
                            dt = g * 4 + j
                            Sc.op("pe", lambda e, dt=dt, j=j, g=g: e.matmul(
                                psL[0][:, :], ones_t[:, 0:1],
                                oaCh_t[:, g * 4 + j, :],
                                start=(dt == 0), stop=(dt == NDT - 1)),
                                reads=[ones_b, oaCh_b], writes=[psL_b[0]])
                            Sc.op("pe", lambda e, dt=dt, j=j, gp_=gp_: e.matmul(
                                psL[1][:, :], ones_t[:, 0:1], sqw_t[gp_][:, j, :],
                                start=(dt == 0), stop=(dt == NDT - 1)),
                                reads=[ones_b, sqw_b[gp_]], writes=[psL_b[1]])
                    rows_from([psL_b[0], psL_b[1]],
                              lambda e: e.tensor_copy(r5_t[:, 6, :], psL[0][:, :]),
                              lambda e: e.tensor_copy(r5_t[:, 7, :], psL[1][:, :]))
                    for g in range(8):
                        gp_ = g % 2
                        Sc.dma("sp", lambda e, g=g, c0=c0, gp_=gp_: e.dma_start(
                            out=xrw_t[gp_][:, :, :],
                            in_=xt_r5[:, g * 4:(g + 1) * 4, c0:c0 + 512]), xrw_b[gp_])
                        Sc.op("dve", lambda e, g=g, gp_=gp_: e.tensor_sub(
                            x1w_t[gp_][:, :, :], oaCh_t[:, g * 4:(g + 1) * 4, :],
                            mB5_t[:, :, :]),
                            reads=[oaCh_b, mB5_b], writes=[x1w_b[gp_]])
                        Sc.op("dve", lambda e, gp_=gp_: e.tensor_mul(
                            x1w_t[gp_][:, :, :], x1w_t[gp_][:, :, :], rB5_t[:, :, :]),
                            reads=[x1w_b[gp_], rB5_b], writes=[x1w_b[gp_]])
                        Sc.op("dve", lambda e, gp_=gp_: e.tensor_add(
                            x1w_t[gp_][:, :, :], x1w_t[gp_][:, :, :], xrw_t[gp_][:, :, :]),
                            reads=[x1w_b[gp_], xrw_b[gp_]], writes=[x1w_b[gp_]])
                        Sc.dma("sp", lambda e, g=g, c0=c0, gp_=gp_: e.dma_start(
                            out=x1s_r[:, g * 4:(g + 1) * 4, c0:c0 + 512],
                            in_=x1w_t[gp_][:, :, :]), dB["x1s"], reads=[x1w_b[gp_]])
                        Sc.op("dve", lambda e, gp_=gp_: e.tensor_copy(
                            bfw_t[gp_][:, :, :], x1w_t[gp_][:, :, :]),
                            reads=[x1w_b[gp_]], writes=[bfw_b[gp_]])
                        Sc.op("act", lambda e, gp_=gp_: e.activation(
                            sqw_t[gp_][:, :, :], bfw_t[gp_][:, :, :], AF.Square),
                            reads=[bfw_b[gp_]], writes=[sqw_b[gp_]])
                        for j in range(4):
                            dt = g * 4 + j
                            Sc.op("pe", lambda e, dt=dt, j=j, gp_=gp_: e.matmul(
                                psL[0][:, :], ones_t[:, 0:1], bfw_t[gp_][:, j, :],
                                start=(dt == 0), stop=(dt == NDT - 1)),
                                reads=[ones_b, bfw_b[gp_]], writes=[psL_b[0]])
                            Sc.op("pe", lambda e, dt=dt, j=j, gp_=gp_: e.matmul(
                                psL[1][:, :], ones_t[:, 0:1], sqw_t[gp_][:, j, :],
                                start=(dt == 0), stop=(dt == NDT - 1)),
                                reads=[ones_b, sqw_b[gp_]], writes=[psL_b[1]])
                    rows_from([psL_b[0], psL_b[1]],
                              lambda e: e.tensor_copy(r5_t[:, 6, :], psL[0][:, :]),
                              lambda e: e.tensor_copy(r5_t[:, 7, :], psL[1][:, :]))
                    for g in range(8):
                        gp_ = g % 2
                        Sc.dma("sp", lambda e, g=g, c0=c0, gp_=gp_: e.dma_start(
                            out=x1w_t[gp_][:, :, :],
                            in_=x1s_r[:, g * 4:(g + 1) * 4, c0:c0 + 512]), x1w_b[gp_],
                            reads=[dB["x1s"]])
                        Sc.op("dve", lambda e, gp_=gp_: e.tensor_sub(
                            xrw_t[gp_][:, :, :], x1w_t[gp_][:, :, :], mB5_t[:, :, :]),
                            reads=[x1w_b[gp_], mB5_b], writes=[xrw_b[gp_]])
                        Sc.op("dve", lambda e, gp_=gp_: e.tensor_mul(
                            bfw_t[gp_][:, :, :], xrw_t[gp_][:, :, :], rB5_t[:, :, :]),
                            reads=[xrw_b[gp_], rB5_b], writes=[bfw_b[gp_]])
                        Sc.dma("sp", lambda e, g=g, c0=c0, gp_=gp_: e.dma_start(
                            out=xns_r[:, g * 4:(g + 1) * 4, c0:c0 + 512],
                            in_=bfw_t[gp_][:, :, :]), dB["xns"], reads=[bfw_b[gp_]])
                        for j in range(4):
                            dt = g * 4 + j
                            Sc.op("pe", lambda e, dt=dt, j=j, gp_=gp_: e.matmul(
                                psE[:, :], enc_t[:, dt, :], bfw_t[gp_][:, j, :],
                                start=(dt == 0), stop=(dt == NDT - 1)),
                                reads=[enc_b, bfw_b[gp_]], writes=[psE_b])
                    Sc.op("dve", lambda e, c0=c0: e.tensor_copy(
                        eh_t[:, c0:c0 + 512], psE[:, :]),
                        reads=[psE_b], writes=[eh_b])
            Sc.barrier()

            x1r_t = [p5.enter_context(nc.sbuf_tensor(f"x1r{i}", [128, 4, 512], F32))
                     for i in range(2)]
            x1r_b = [Buf(f"x1r{i}") for i in range(2)]
            ow_t = [p5.enter_context(nc.sbuf_tensor(f"ow{i}", [128, 4, 512], F32))
                    for i in range(2)]
            ow_b = [Buf(f"ow{i}") for i in range(2)]
            yw_t = [p5.enter_context(nc.sbuf_tensor(f"yw{i}", [128, 4, 512], BF16))
                    for i in range(2)]
            yw_b = [Buf(f"yw{i}") for i in range(2)]

            def emit_p5c(ch):
                c0 = ch * 512
                rows_from([st4_b],
                          lambda e, ch=ch: e.tensor_copy(r5_t[:, 6, :],
                                                         st4_t[:, 0, ch, :]),
                          lambda e, ch=ch: e.tensor_copy(r5_t[:, 7, :],
                                                         st4_t[:, 1, ch, :]))
                for g in range(8):
                    gp_ = g % 2
                    Sc.dma("gp", lambda e, g=g, c0=c0, gp_=gp_: e.dma_start(
                        out=x1r_t[gp_][:, :, :],
                        in_=x1s_r[:, g * 4:(g + 1) * 4, c0:c0 + 512]), x1r_b[gp_],
                        reads=[dB["x1s"]])
                    Sc.dma("gp", lambda e, g=g, c0=c0, gp_=gp_: e.dma_start(
                        out=yw_t[gp_][:, :, :],
                        in_=ys_r[:, g * 4:(g + 1) * 4, c0:c0 + 512]), yw_b[gp_],
                        reads=[dB["ys"]])
                    Sc.op("dve", lambda e, gp_=gp_: e.tensor_sub(
                        ow_t[gp_][:, :, :], yw_t[gp_][:, :, :], mB5_t[:, :, :]),
                        reads=[yw_b[gp_], mB5_b], writes=[ow_b[gp_]])
                    Sc.op("dve", lambda e, gp_=gp_: e.tensor_mul(
                        ow_t[gp_][:, :, :], ow_t[gp_][:, :, :], rB5_t[:, :, :]),
                        reads=[ow_b[gp_], rB5_b], writes=[ow_b[gp_]])
                    Sc.op("dve", lambda e, gp_=gp_: e.tensor_add(
                        ow_t[gp_][:, :, :], ow_t[gp_][:, :, :], x1r_t[gp_][:, :, :]),
                        reads=[ow_b[gp_], x1r_b[gp_]], writes=[ow_b[gp_]])
                    Sc.dma("gp", lambda e, g=g, c0=c0, gp_=gp_: e.dma_start(
                        out=out_r[:, g * 4:(g + 1) * 4, c0:c0 + 512],
                        in_=ow_t[gp_][:, :, :]), dB["out"], reads=[ow_b[gp_]])

            with ExitStack() as p5b:
                xnh_t = p5b.enter_context(nc.sbuf_tensor("xnh", [128, NDT, 1024], BF16))
                xnh_b = Buf("xnh")
                wb5_t = [p5b.enter_context(nc.sbuf_tensor(f"wb5{i}", [128, NDT, 128],
                                                          BF16)) for i in range(2)]
                wb5_b = [Buf(f"wb5{i}") for i in range(2)]
                psY = [p5b.enter_context(nc.psum_tensor(f"psY{i}", [128, 512], F32))
                       for i in range(4)]
                psY_b = [Buf(f"psY{i}") for i in range(4)]
                psT = [p5b.enter_context(nc.psum_tensor(f"psT{i}", [1, 512], F32))
                       for i in range(2)]
                psT_b = [Buf(f"psT{i}") for i in range(2)]
                sq2_t = [p5b.enter_context(nc.sbuf_tensor(f"sq2{i}", [128, 512], BF16))
                         for i in range(2)]
                sq2_b = [Buf(f"sq2{i}") for i in range(2)]
                yst_t = [p5b.enter_context(nc.sbuf_tensor(f"yst{i}", [128, 512], BF16))
                         for i in range(4)]
                yst_b = [Buf(f"yst{i}") for i in range(4)]
                wst_t = [p5b.enter_context(nc.sbuf_tensor(f"wst{i}", [128, 16, 128],
                                                          BF16)) for i in range(2)]
                wst_b = [Buf(f"wst{i}") for i in range(2)]
                Sc.op("dve", lambda e: e.memset(st4_t[:, :, :, :], 0.0),
                      writes=[st4_b])
                back_y = []

                def y_group(hf, ot, qn):
                    par, pp = ot % 2, (ot * 2 + qn) % 4
                    q0 = hf * 1024 + qn * 512
                    ql = qn * 512
                    for dt in range(NDT):
                        Sc.op("pe", lambda e, dt=dt, ql=ql, par=par, pp=pp: e.matmul(
                            psY[pp][:, :], wb5_t[par][:, dt, :],
                            xnh_t[:, dt, ql:ql + 512],
                            start=(dt == 0), stop=False),
                            reads=[wb5_b[par], xnh_b], writes=[psY_b[pp]])
                    Sc.op("pe", lambda e, ot=ot, q0=q0, pp=pp: e.matmul(
                        psY[pp][:, :], gt_t[0:40, ot * 128:(ot + 1) * 128],
                        eh_t[:, q0:q0 + 512], start=False, stop=True),
                        reads=[gt_b, eh_b], writes=[psY_b[pp]])
                    back_y.append((hf, ot, qn, pp))

                def y_back():
                    hf, ot, qn, pp = back_y.pop(0)
                    q0 = hf * 1024 + qn * 512
                    sp_ = (ot * 2 + qn) % 2
                    ysl = (ot * 2 + qn) % 4
                    qn4 = hf * 2 + qn
                    Sc.op("act", lambda e, pp=pp, ot=ot, ysl=ysl: e.activation(
                        yst_t[ysl][:, :], psY[pp][:, :], AF.Identity,
                        bias=bt_b[0][:, ot:ot + 1]),
                        reads=[psY_b[pp], bt_b[1]], writes=[yst_b[ysl]])
                    Sc.dma("sp", lambda e, ot=ot, q0=q0, ysl=ysl: e.dma_start(
                        out=ys_r[:, ot, q0:q0 + 512], in_=yst_t[ysl][:, :]),
                        dB["ys"], reads=[yst_b[ysl]])
                    Sc.op("act", lambda e, sp_=sp_, ysl=ysl: e.activation(
                        sq2_t[sp_][:, :], yst_t[ysl][:, :], AF.Square),
                        reads=[yst_b[ysl]], writes=[sq2_b[sp_]])
                    Sc.op("pe", lambda e, ysl=ysl: e.matmul(
                        psT[0][:, :], ones_t[:, 0:1], yst_t[ysl][:, :],
                        start=True, stop=True),
                        reads=[ones_b, yst_b[ysl]], writes=[psT_b[0]])
                    Sc.op("pe", lambda e, sp_=sp_: e.matmul(
                        psT[1][:, :], ones_t[:, 0:1], sq2_t[sp_][:, :],
                        start=True, stop=True),
                        reads=[ones_b, sq2_b[sp_]], writes=[psT_b[1]])
                    Sc.op("dve", lambda e, qn4=qn4: e.tensor_add(
                        st4_t[:, 0, qn4, :], st4_t[:, 0, qn4, :], psT[0][:, :]),
                        reads=[psT_b[0], st4_b], writes=[st4_b])
                    Sc.op("dve", lambda e, qn4=qn4: e.tensor_add(
                        st4_t[:, 1, qn4, :], st4_t[:, 1, qn4, :], psT[1][:, :]),
                        reads=[psT_b[1], st4_b], writes=[st4_b])

                for hf in range(2):
                    for g in range(4):
                        Sc.dma("sp", lambda e, g=g, hf=hf: e.dma_start(
                            out=xnh_t[:, g * 8:(g + 1) * 8, :],
                            in_=xns_r[:, g * 8:(g + 1) * 8,
                                      hf * 1024:(hf + 1) * 1024]), xnh_b,
                            reads=[dB["xns"]])
                    for par in range(2):
                        for g in range(2):
                            if hf == 0:
                                Sc.dma("gp", lambda e, par=par, g=g: e.dma_start(
                                    out=wb5_t[par][:, g * 16:(g + 1) * 16, :],
                                    in_=wbt_r[:, g * 16:(g + 1) * 16,
                                              par * 128:(par + 1) * 128]), wb5_b[par])
                            else:
                                Sc.dma("sp", lambda e, par=par, g=g: e.dma_start(
                                    out=wb5_t[par][:, g * 16:(g + 1) * 16, :],
                                    in_=wbc_r[:, g * 16:(g + 1) * 16,
                                              par * 128:(par + 1) * 128]), wb5_b[par],
                                    reads=[dB["wbc"]])
                    for ot in range(NDT):
                        if hf == 0:
                            for g in range(2):
                                Sc.dma("sp", lambda e, ot=ot, g=g: e.dma_start(
                                    out=wbc_r[:, g * 16:(g + 1) * 16,
                                              ot * 128:(ot + 1) * 128],
                                    in_=wst_t[0][:, :, :][:, g * 16:(g + 1) * 16, :]
                                    if False else wb5_t[ot % 2][:, g * 16:(g + 1) * 16, :]),
                                    dB["wbc"], reads=[wb5_b[ot % 2]])
                        for qn in range(2):
                            if len(back_y) >= 2:
                                y_back()
                            y_group(hf, ot, qn)
                        if ot + 2 < NDT:
                            oo = ot + 2
                            for g in range(2):
                                if hf == 0:
                                    Sc.dma("gp", lambda e, oo=oo, g=g: e.dma_start(
                                        out=wb5_t[oo % 2][:, g * 16:(g + 1) * 16, :],
                                        in_=wbt_r[:, g * 16:(g + 1) * 16,
                                                  oo * 128:(oo + 1) * 128]),
                                        wb5_b[oo % 2])
                                else:
                                    Sc.dma("sp", lambda e, oo=oo, g=g: e.dma_start(
                                        out=wb5_t[oo % 2][:, g * 16:(g + 1) * 16, :],
                                        in_=wbc_r[:, g * 16:(g + 1) * 16,
                                                  oo * 128:(oo + 1) * 128]),
                                        wb5_b[oo % 2], reads=[dB["wbc"]])
                    while back_y:
                        y_back()
                    for ch in (0, 1) if hf == 0 else (2, 3):
                        emit_p5c(ch)

        Sc.op("sp", lambda e: e.nop(), reads=[dB["out"]])

        print("kernel build stats:", Sc.stats())
        with nc.Block() as block:
            Sc.emit(block)
    return nc



def _prep_shared(ins):
    a = {}
    a["wdq"] = np.ascontiguousarray(ins["L0_attn_w_dq"], np.float32)
    a["wdkv"] = np.ascontiguousarray(ins["L0_attn_w_dkv"], np.float32)
    wrk = np.asarray(ins["L0_attn_w_rk"], np.float32)
    a["wrk2"] = np.ascontiguousarray(np.concatenate([wrk, wrk], axis=1))
    a["wuq"] = np.ascontiguousarray(ins["L0_attn_w_uq"], np.float32)
    wrq = np.asarray(ins["L0_attn_w_rq"], np.float32)
    wrq2 = np.empty((DC1, D), np.float32)
    for h in range(H):
        blk = wrq[:, h * 64:(h + 1) * 64]
        wrq2[:, h * 128:h * 128 + 64] = blk
        wrq2[:, h * 128 + 64:(h + 1) * 128] = blk
    a["wrq2"] = wrq2
    a["wuk"] = np.ascontiguousarray(ins["L0_attn_w_uk"], np.float32)
    a["wuv"] = np.ascontiguousarray(ins["L0_attn_w_uv"], np.float32)
    a["wout"] = np.ascontiguousarray(ins["L0_attn_w_out"], np.float32)
    a["wbt"] = np.ascontiguousarray(np.asarray(ins["L0_ffn_W_base"], np.float32).T)
    a["enct"] = np.ascontiguousarray(np.concatenate(
        [np.asarray(ins["L0_ffn_enc0"], np.float32).T,
         np.asarray(ins["L0_ffn_enc1"], np.float32).T], axis=1))
    a["co0"] = np.ascontiguousarray(
        np.transpose(np.asarray(ins["L0_ffn_coefs0"], np.float32), (1, 0, 2)).reshape(D, 96))
    a["co1"] = np.ascontiguousarray(
        np.transpose(np.asarray(ins["L0_ffn_coefs1"], np.float32), (1, 0, 2)).reshape(D, 24))
    a["mk0"] = np.ascontiguousarray(
        np.transpose(np.asarray(ins["L0_ffn_mask0"]), (1, 0, 2)).reshape(D, 96)).astype(np.uint8)
    a["mk1"] = np.ascontiguousarray(
        np.transpose(np.asarray(ins["L0_ffn_mask1"]), (1, 0, 2)).reshape(D, 24)).astype(np.uint8)
    a["mt"] = _fourier_mt()
    a["bdq"] = _btile(np.asarray(ins["L0_attn_b_dq"], np.float32))
    a["bdkv"] = _btile(np.asarray(ins["L0_attn_b_dkv"], np.float32))
    a["brk2"] = _btile64dup(np.asarray(ins["L0_attn_b_rk"], np.float32))
    a["buq"] = _btile(np.asarray(ins["L0_attn_b_uq"], np.float32))
    brq = np.asarray(ins["L0_attn_b_rq"], np.float32).reshape(H, 64)
    a["brq2"] = np.ascontiguousarray(
        np.concatenate([brq, brq], axis=1).T)
    a["buk"] = _btile(np.asarray(ins["L0_attn_b_uk"], np.float32))
    a["buv"] = _btile(np.asarray(ins["L0_attn_b_uv"], np.float32))
    a["bout"] = _btile(np.asarray(ins["L0_attn_b_out"], np.float32))
    a["bbase"] = _btile(np.asarray(ins["L0_ffn_b_base"], np.float32))
    a["identc"] = np.eye(128, dtype=np.float32).astype(bfdt)
    a["onesc"] = np.ones((128, 128), np.float32).astype(bfdt)
    return a


def kernel(**ins):
    x = np.asarray(ins["L0_attn_sequence"], np.float32)
    shared = _prep_shared(ins)
    nc = build_program()
    in_maps = []
    for c in range(NCORES):
        b, qi = c // 4, c % 4
        xtb = np.ascontiguousarray(x[b].T)
        qo = qi * QOWN
        perm = np.concatenate([xtb[:, qo:qo + QOWN], xtb[:, :qo], xtb[:, qo + QOWN:]],
                              axis=1)
        m = dict(shared)
        m["xt"] = np.ascontiguousarray(perm)
        in_maps.append(m)
    res = run_bass_kernel_spmd(nc, in_maps, core_ids=list(range(NCORES)))
    outp = np.empty((B, S, D), np.float32)
    for c in range(NCORES):
        b, qi = c // 4, c % 4
        outp[b, qi * QOWN:(qi + 1) * QOWN, :] = res.results[c]["out"].T
    return outp
